# version 01 (final)
# baseline (speedup 1.0000x reference)
import contextlib
import math
import numpy as np
import ml_dtypes
import concourse.bass as bass
import concourse.mybir as mybir
from concourse.bass import IndirectOffsetOnAxis
from concourse.bass_utils import run_bass_kernel_spmd

F32 = mybir.dt.float32
BF16 = mybir.dt.bfloat16
I32 = mybir.dt.int32
AF = mybir.ActivationFunctionType
ALU = mybir.AluOpType
AX = mybir.AxisListType

D = 1024
B = 4
S = 4096
NCORE = 8
T = 2048
NC = 512
FC = D // 128
EPS = 1e-5
bf16 = ml_dtypes.bfloat16

ENGINES = ("sync", "tensor", "vector", "scalar", "gpsimd")


class Sem:
    def __init__(self, h):
        self.h = h
        self.n = 0


class Prog:

    def __init__(self, name="prog"):
        self.nc = bass.Bass("TRN2", target_bir_lowering=False)
        self.st = contextlib.ExitStack()
        self.q = {e: [] for e in ENGINES}

    def sb(self, name, shape, dt):
        return self.st.enter_context(self.nc.sbuf_tensor(name, list(shape), dt))

    def sb_cached(self, name, shape, dt):
        if not hasattr(self, "_sbc"):
            self._sbc = {}
        key = (name, tuple(shape), dt)
        if key not in self._sbc:
            self._sbc[key] = self.sb(name, shape, dt)
        return self._sbc[key]

    def psum(self, name, shape, dt=F32):
        return self.st.enter_context(self.nc.psum_tensor(name, list(shape), dt))

    def sem(self, name):
        return Sem(self.st.enter_context(self.nc.semaphore(name)))

    def op(self, eng, emit, waits=(), inc=None, k=1):
        wl = [(s.h, tgt) for (s, tgt) in waits]
        ih = None
        if inc is not None:
            inc.n += k
            ih, ik = inc.h, k

        def f(E, wl=wl, emit=emit, ih=ih, ik=(k if inc is not None else 0)):
            for h, tgt in wl:
                E.wait_ge(h, tgt)
            inst = emit(E)
            if ih is not None:
                inst.then_inc(ih, ik)
        self.q[eng].append(f)

    def wait(self, eng, *waits):
        wl = [(s.h, tgt) for (s, tgt) in waits]

        def f(E, wl=wl):
            for h, tgt in wl:
                E.wait_ge(h, tgt)
        self.q[eng].append(f)

    def chain(self, eng, sem, emit, extra_waits=()):
        self.op(eng, emit, waits=[(sem, sem.n)] + list(extra_waits), inc=sem)

    def build(self):
        block = self.st.enter_context(self.nc.Block())
        for eng in ENGINES:
            fns = self.q[eng]
            if not fns:
                continue
            getattr(block, eng)(lambda E, fns=fns: [f(E) for f in fns])
        self.st.close()
        return self.nc



def make_consts():
    c = {}
    c["ones_col"] = np.ones((128, 1), np.float32)
    c["ones_row"] = np.ones((1, 128), np.float32)
    c["ones_rowN"] = np.ones((1, 512), np.float32)
    c["ones_rowN_bf"] = np.ones((1, 512), np.float32).astype(bf16)
    c["ident_bf"] = np.eye(128, dtype=np.float32).astype(bf16)
    c["ident_f32"] = np.eye(128, dtype=np.float32)
    c["iota8"] = np.broadcast_to(np.arange(8, dtype=np.float32), (128, 8)).copy()
    c["iota128"] = np.arange(128, dtype=np.float32).reshape(128, 1)
    c["lt_strict"] = np.triu(np.ones((128, 128), np.float32), 1).astype(bf16)
    return c


CONST_SHAPES = {"ones_col": ([128, 1], F32), "ones_row": ([1, 128], F32),
                "iota128": ([128, 1], F32), "ones_rowN_bf": ([1, NC], BF16),
                "ones_rowN": ([1, NC], F32),
                "ident_bf": ([128, 128], BF16), "ident_f32": ([128, 128], F32),
                "iota8": ([128, 8], F32),
                "lt_strict": ([128, 128], BF16)}


class Consts:
    def __init__(self, p, ld_sem, which=("ones_col", "ones_row", "ident_bf")):
        self.names = which
        self.sb = {}
        for w in which:
            sh, dt = CONST_SHAPES[w]
            dr = p.nc.dram_tensor("c_" + w, sh, dt, kind="ExternalInput")
            t = p.sb("s_" + w, sh, dt)
            self.sb[w] = t
            p.op("sync", lambda sy, t=t, dr=dr: sy.dma_start(out=t[:, :], in_=dr[:, :]),
                 inc=ld_sem, k=16)

    def __getitem__(self, k):
        return self.sb[k]


def feed_consts(m, names, consts):
    for w in names:
        m["c_" + w] = consts[w]



def barrier(p, sems, which=("pe", "act", "dve")):
    return [(sems[k], sems[k].n) for k in which]


def emit_matmul(p, tag, w_dram, K, M, xn_sb, xn_W, out_cb, sems, pp,
                wf, wb, xn_off=0, width=None, pre=(), wslice=None,
                bias_row=None, ones_rowN=None):
    width = width or xn_W
    nch = width // NC
    MC = M // 128
    KC = K // 128
    pe, dve, ld = sems["pe"], sems["dve"], sems["ld"]
    ps_free = {0: [], 1: []}
    prev_conv = None
    prev_mm = None
    for mc in range(MC):
        src = wslice(mc) if wslice else w_dram[mc, :, :]
        lw = list(pre) if mc == 0 else [(dve, prev_conv)]
        p.op("sync", lambda sy, o=wf, src=src: sy.dma_start(out=o[:, 0:K], in_=src),
             waits=lw, inc=ld, k=16)
        ld_t = ld.n
        cw = [(ld, ld_t)] + (list(pre) if mc == 0 else [(pe, prev_mm)])
        p.chain("vector", dve, lambda v, o=wb, a=wf: v.tensor_copy(o[:, 0:K], a[:, 0:K]),
                extra_waits=cw)
        prev_conv = dve.n
        brow = bias_row(mc) if bias_row is not None else None
        for i in range(nch):
            pj = (mc * nch + i) % 2
            ps = pp["mm0"] if pj == 0 else pp["mm1"]
            waits = [(dve, prev_conv)] + ps_free[pj] + (list(pre) if mc == 0 and i == 0 else [])
            for kc in range(KC):
                lhsT = wb[:, kc * 128:(kc + 1) * 128]
                rhs = xn_sb[:, kc * xn_W + xn_off + i * NC: kc * xn_W + xn_off + (i + 1) * NC]
                p.op("tensor",
                     lambda t, lhsT=lhsT, rhs=rhs, ps=ps, f0=(kc == 0),
                     f1=(kc == KC - 1 and brow is None):
                     t.matmul(ps[:, :], lhsT, rhs, start=f0, stop=f1),
                     waits=(waits if kc == 0 else []),
                     inc=(pe if (kc == KC - 1 and brow is None) else None))
            if brow is not None:
                p.op("tensor", lambda t, ps=ps, br=brow, onr=ones_rowN:
                     t.matmul(ps[:, :], br, onr[0:1, 0:NC], start=False, stop=True),
                     inc=pe)
            fr = out_cb(mc, i, ps, [(pe, pe.n)])
            ps_free[pj] = fr
        prev_mm = pe.n


def act_evac(p, dst_ap, func=AF.Copy, bias=0.0, scale=1.0, sems=None):
    act = sems["act"]

    def cb(mc, i, ps, waits):
        b = bias(mc) if callable(bias) else bias
        sc = scale(mc) if callable(scale) else scale
        p.chain("scalar", act,
                lambda s, o=dst_ap(mc, i), ps=ps, b=b, sc=sc:
                s.activation(o, ps[:, :], func, bias=b, scale=sc),
                extra_waits=waits)
        return [(act, act.n)]
    return cb



def fm_view(sb_t, fc, W, i0, n):
    return sb_t[:, fc * W + i0: fc * W + i0 + n]


def std_pp(p):
    return {"st0": p.psum("pst0", [1, NC]), "st1": p.psum("pst1", [1, NC]),
            "bc0": p.psum("pbc0", [128, NC]), "bc1": p.psum("pbc1", [128, NC]),
            "mm0": p.psum("pmm0", [128, NC]), "mm1": p.psum("pmm1", [128, NC])}


def fm_dram_ap(dr):
    return dr.ap().rearrange("(f p) t -> p f t", p=128)


def fm_sb_ap(sb_t, W, ncol=None):
    return sb_t[:, 0:(ncol or (FC * W))].rearrange("p (f t) -> p f t", t=W)


def start_frame(p, W=T, with_contrib=False, contrib_matmul=None):
    nc = p.nc
    sems = {k: p.sem(k) for k in ("pe", "act", "dve", "ld", "st", "cl", "gp")}
    pp = std_pp(p)
    x_d = nc.dram_tensor("x", [D, W], F32, kind="ExternalInput")
    y_d = nc.dram_tensor("y", [D, W], F32, kind="ExternalOutput")
    x_sb = p.sb("x_sb", [128, FC * W], F32)
    p.op("sync", lambda sy: sy.dma_start(out=fm_sb_ap(x_sb, W), in_=fm_dram_ap(x_d)),
         inc=sems["ld"], k=16)
    return sems, pp, x_d, y_d, x_sb


def finish_frame(p, x_sb, y_d, W, sems, waits):
    p.op("gpsimd", lambda g: g.dma_start(out=fm_dram_ap(y_d), in_=fm_sb_ap(x_sb, W)),
         waits=waits, inc=sems["st"], k=16)
    p.wait("gpsimd", (sems["st"], sems["st"].n))



def build_ffn3():
    p = Prog("ffn3")
    nc = p.nc
    DH = 4096
    DHC = DH // 128
    W = 512
    sems, pp, x_d, y_d, x_sb = start_frame(p)
    C = Consts(p, sems["cl"], which=("ones_col", "ones_row"))
    w_d = {k: nc.dram_tensor(k, [mm // 128, 128, kk], F32, kind="ExternalInput")
           for k, kk, mm in (("w1", D, DH), ("w2", D, DH), ("h1", DH, DH),
                             ("h2", DH, DH), ("w3", DH, D))}
    s_d = {k: nc.dram_tensor("s" + k, [128, dim // 128], F32, kind="ExternalInput")
           for k, dim in (("1", DH), ("2", DH), ("h1", DH), ("h2", DH), ("3", D))}

    sc_sb = {}
    for k, dim in (("1", DH), ("2", DH), ("h1", DH), ("h2", DH), ("3", D)):
        t = p.sb("sc" + k, [128, dim // 128], F32)
        sc_sb[k] = t
        p.op("sync", lambda sy, t=t, d=s_d[k]: sy.dma_start(out=t[:, :], in_=d[:, :]),
            inc=sems["ld"], k=16)
    ld_t = sems["ld"].n
    for e in ("vector", "tensor", "scalar"):
        p.wait(e, (sems["ld"], ld_t), (sems["cl"], sems["cl"].n))

    s3h = p.sb("s3h", [128, FC], F32)
    p.chain("vector", sems["dve"],
            lambda v, o=s3h, a=sc_sb["3"]: v.tensor_scalar(o[:, :], a[:, :], 0.5, None, ALU.mult))

    xn = p.sb("xn3", [128, FC * W], BF16)
    h1b = p.sb("h1b", [128, DHC * W], BF16)
    h2b = p.sb("h2b", [128, DHC * W], BF16)
    tmp = p.sb("tmp3", [128, W], F32)
    tmpb = p.sb("tmpb", [128, W], BF16)
    wf = p.sb("wf", [128, DH], F32)
    wb = p.sb("wb", [128, DH], BF16)

    for q in range(4):
        off = q * W
        emit_ln_q(p, f"ln{q}", x_sb, T, off, W, xn, C, pp, sems)

        emit_dual_gate(p, q, w_d["w1"], w_d["w2"], sc_sb["1"], sc_sb["2"], D, DH,
                       xn, W, h1b, sems, pp, wf, wb, tmp, tmpb,
                       pre=barrier(p, sems))

        emit_matmul(p, f"h1_{q}", w_d["h1"], DH, DH, h1b, W,
                    act_evac(p, lambda mc, i: fm_view(h2b, mc, W, i * NC, NC),
                             func=AF.Gelu, scale=lambda mc: sc_sb["h1"][:, mc:mc + 1],
                             sems=sems), sems, pp, wf, wb, pre=barrier(p, sems))
        emit_matmul(p, f"h2_{q}", w_d["h2"], DH, DH, h2b, W,
                    act_evac(p, lambda mc, i: fm_view(h1b, mc, W, i * NC, NC),
                             func=AF.Gelu, scale=lambda mc: sc_sb["h2"][:, mc:mc + 1],
                             sems=sems), sems, pp, wf, wb, pre=barrier(p, sems))

        def cb(mc, i, ps, waits, off=off):
            xa = x_sb[:, mc * T + off + i * NC: mc * T + off + (i + 1) * NC]
            p.chain("scalar", sems["act"],
                    lambda s, o=tmp, ps=ps, sc=s3h[:, mc:mc + 1]:
                    s.activation(o[:, 0:NC], ps[:, :], AF.Copy, scale=sc),
                    extra_waits=waits)
            a_t = sems["act"].n
            p.chain("vector", sems["dve"],
                    lambda v, xa=xa, o=tmp: v.tensor_add(xa, xa, o[:, 0:NC]),
                    extra_waits=[(sems["act"], a_t)])
            return [(sems["dve"], sems["dve"].n)]
        emit_matmul(p, f"w3_{q}", w_d["w3"], DH, D, h1b, W, cb, sems, pp,
                    wf, wb, pre=barrier(p, sems))

    finish_frame(p, x_sb, y_d, T, sems, barrier(p, sems))
    return p.build()


def emit_dual_gate(p, q, w1_d, w2_d, sc1, sc2, K, M, xn, W, out_sb, sems, pp,
                   wf, wb, tmp, tmpb, pre=()):
    MC = M // 128
    KC = K // 128
    pe, act, dve, ld = sems["pe"], sems["act"], sems["dve"], sems["ld"]
    prev_conv = None
    prev_mm = None
    ps_free = []
    for mc in range(MC):
        lw = list(pre) if mc == 0 else [(dve, prev_conv)]
        p.op("sync", lambda sy, o=wf, src=w1_d[mc, :, :], K=K: sy.dma_start(
            out=o[:, 0:K], in_=src), waits=lw, inc=ld, k=16)
        p.op("sync", lambda sy, o=wf, src=w2_d[mc, :, :], K=K: sy.dma_start(
            out=o[:, K:2 * K], in_=src), inc=ld, k=16)
        ld_t = ld.n
        cw = [(ld, ld_t)] + (list(pre) if mc == 0 else [(pe, prev_mm)])
        p.chain("vector", dve, lambda v, o=wb, a=wf, K=K: v.tensor_copy(
            o[:, 0:2 * K], a[:, 0:2 * K]), extra_waits=cw)
        prev_conv = dve.n
        for which, base, ps in ((0, 0, pp["mm0"]), (1, K, pp["mm1"])):
            waits = [(dve, prev_conv)] + ps_free + (list(pre) if mc == 0 else [])
            for kc in range(KC):
                lhsT = wb[:, base + kc * 128: base + (kc + 1) * 128]
                rhs = xn[:, kc * W: kc * W + NC]
                p.op("tensor",
                     lambda t, lhsT=lhsT, rhs=rhs, ps=ps, f0=(kc == 0), f1=(kc == KC - 1):
                     t.matmul(ps[:, :], lhsT, rhs, start=f0, stop=f1),
                     waits=(waits if kc == 0 else []),
                     inc=(pe if kc == KC - 1 else None))
        mm_t = pe.n
        p.chain("scalar", act, lambda s, o=tmpb, ps=pp["mm0"], sc=sc1[:, mc:mc + 1]:
                s.activation(o[:, 0:NC], ps[:, :], AF.Silu, scale=sc),
                extra_waits=[(pe, mm_t)])
        p.chain("scalar", act, lambda s, o=tmp, ps=pp["mm1"], sc=sc2[:, mc:mc + 1]:
                s.activation(o[:, 0:NC], ps[:, :], AF.Copy, scale=sc))
        a_t = act.n
        p.chain("vector", dve,
                lambda v, o=fm_view(out_sb, mc, W, 0, NC), a=tmpb, b=tmp:
                v.tensor_mul(o, a[:, 0:NC], b[:, 0:NC]),
                extra_waits=[(act, a_t)])
        ps_free = [(act, a_t)]
        prev_mm = pe.n


def emit_ln_q(p, tag, x_sb, XW, off, W, xn_sb, C, pp, sems, affine=None, xn_out=None):
    if xn_out is not None:
        xo_buf, xo_off, xo_XW = xn_out
    else:
        xo_buf, xo_off, xo_XW = xn_sb, 0, W
    allsems = [sems[k] for k in ("pe", "act", "dve")]
    orig_op = p.op

    def strict_op(eng, emit, waits=(), inc=None, k=1):
        orig_op(eng, emit, waits=list(waits) + [(s, s.n) for s in allsems], inc=inc, k=k)
    p.op = strict_op
    try:
        _emit_ln_q_body(p, tag, x_sb, XW, off, W, C, pp, sems, affine,
                        xo_buf, xo_off, xo_XW)
    finally:
        p.op = orig_op
    return None


def _emit_ln_q_body(p, tag, x_sb, XW, off, W, C, pp, sems, affine,
                    xo_buf, xo_off, xo_XW):
    nch = W // NC
    pe, act, dve = sems["pe"], sems["act"], sems["dve"]
    st = p.sb_cached("ln_st", [1, W * 3], F32)
    sq = [p.sb_cached("ln_sq0", [128, NC], F32), p.sb_cached("ln_sq1", [128, NC], F32)]
    prev_stats_read = 0

    for i in range(nch):
        ar = st[0:1, 0 * W + i * NC: 0 * W + (i + 1) * NC]
        br = st[0:1, 1 * W + i * NC: 1 * W + (i + 1) * NC]
        mr = st[0:1, 2 * W + i * NC: 2 * W + (i + 1) * NC]
        xcs = [x_sb[:, fc * XW + off + i * NC: fc * XW + off + (i + 1) * NC]
               for fc in range(FC)]
        last_sq_reader = [None, None]

        pe_consumed = []
        for fc in range(FC):
            j = fc % 2
            ew = [(pe, pe_consumed[fc - 2])] if fc >= 2 else []
            if last_sq_reader[j] is not None:
                ew = ew + [last_sq_reader[j]]
            p.chain("scalar", act,
                    lambda s, a=sq[j], b=xcs[fc]: s.activation(a[:, :], b, AF.Square),
                    extra_waits=ew)
            a_t = act.n
            p.op("tensor", lambda t, b=xcs[fc], o=pp["st0"], f0=(fc == 0),
                 f1=(fc == FC - 1), oc=C["ones_col"]: t.matmul(
                     o[0:1, :], oc[:, 0:1], b, start=f0, stop=f1),
                 waits=([(dve, prev_stats_read)] if fc == 0 else []))
            p.op("tensor", lambda t, a=sq[j], o=pp["st1"], f0=(fc == 0),
                 f1=(fc == FC - 1), oc=C["ones_col"]: t.matmul(
                     o[0:1, :], oc[:, 0:1], a[:, :], start=f0, stop=f1),
                 waits=[(act, a_t)] + ([(dve, prev_stats_read)] if fc == 0 else []), inc=pe)
            pe_consumed.append(pe.n)
        pe_done = pe.n

        ch = lambda emit, ew=(): p.chain("vector", dve, emit, extra_waits=list(ew))
        ch(lambda v, o=mr, s=pp["st0"]: v.tensor_scalar(o, s[0:1, :], 1.0 / D, None, ALU.mult),
           ew=[(pe, pe_done)])
        ch(lambda v, o=ar, s=pp["st1"]: v.tensor_scalar(o, s[0:1, :], 1.0 / D, None, ALU.mult))
        prev_stats_read = dve.n
        ch(lambda v, o=br, a=mr: v.tensor_mul(o, a, a))
        ch(lambda v, o=ar, b=br: v.tensor_sub(o, o, b))
        ch(lambda v, o=br, a=ar: v.tensor_scalar(o, a, EPS, None, ALU.add))
        p.chain("scalar", act, lambda s, o=br: s.activation(o, o, AF.Sqrt),
                extra_waits=[(dve, dve.n)])
        ch(lambda v, o=br: v.reciprocal(o, o), ew=[(act, act.n)])
        if affine is None:
            ch(lambda v, o=ar, m=mr, b=br: v.tensor_mul(o, m, b))
            ch(lambda v, o=mr, b=br: v.tensor_copy(o, b))
            ch(lambda v, o=br, a=ar: v.tensor_copy(o, a))
            ch(lambda v, o=ar, m=mr: v.tensor_copy(o, m))
        else:
            ch(lambda v, o=ar, b=br: v.tensor_mul(o, o, b))
            ch(lambda v, o=ar, b=br: v.tensor_mul(o, o, b))
            ch(lambda v, o=ar: v.tensor_scalar(o, o, EPS, None, ALU.add))
            p.chain("scalar", act, lambda s, o=ar: s.activation(o, o, AF.Sqrt),
                    extra_waits=[(dve, dve.n)])
            ch(lambda v, o=ar: v.reciprocal(o, o), ew=[(act, act.n)])
            ch(lambda v, o=ar, b=br: v.tensor_mul(o, o, b))
            ch(lambda v, o=br, m=mr, a=ar: v.tensor_mul(o, m, a))
        dve_done = dve.n

        p.op("tensor", lambda t, o=pp["bc0"], orr=C["ones_row"], a=ar: t.matmul(
            o[:, :], orr[0:1, :], a, start=True, stop=True), waits=[(dve, dve_done)])
        p.op("tensor", lambda t, o=pp["bc1"], orr=C["ones_row"], b=br: t.matmul(
            o[:, :], orr[0:1, :], b, start=True, stop=True), inc=pe)
        bc_done = pe.n

        for fc in range(FC):
            j = fc % 2
            dst = xo_buf[:, fc * xo_XW + xo_off + i * NC: fc * xo_XW + xo_off + (i + 1) * NC]
            p.chain("vector", dve,
                    lambda v, a=sq[j], b=xcs[fc], bc=pp["bc0"]: v.tensor_mul(a[:, :], b, bc[:, :]),
                    extra_waits=[(pe, bc_done)])
            if affine is None:
                p.chain("vector", dve,
                        lambda v, o=dst, a=sq[j], bc=pp["bc1"]: v.tensor_sub(o, a[:, :], bc[:, :]))
                last_sq_reader[j] = (dve, dve.n)
            else:
                g_col, b_col = affine
                p.chain("vector", dve,
                        lambda v, a=sq[j], bc=pp["bc1"]: v.tensor_sub(a[:, :], a[:, :], bc[:, :]))
                p.chain("vector", dve,
                        lambda v, o=dst, a=sq[j], g=g_col[:, fc:fc + 1],
                        bb=b_col[:, fc:fc + 1]: v.tensor_scalar(
                            o, a[:, :], g, bb, ALU.mult, ALU.add))
                last_sq_reader[j] = (dve, dve.n)
    return st


def mblocks(Wm):
    K, M = Wm.shape
    r = Wm.reshape(K // 128, 128, M // 128, 128).transpose(2, 1, 0, 3).reshape(M // 128, 128, K)
    return np.ascontiguousarray(r)


def prep_ffn3(inp):
    d = {}
    d["w1"] = mblocks(inp["L3_ffn_w1_w"].T)
    d["w2"] = mblocks(inp["L3_ffn_w2_w"].T)
    d["h1"] = mblocks(inp["L3_ffn_h1_w"].T)
    d["h2"] = mblocks(inp["L3_ffn_h2_w"].T)
    d["w3"] = mblocks(inp["L3_ffn_w3_w"].T)
    col = lambda s: np.ascontiguousarray(s.reshape(-1, 128).T)
    d["s1"] = col(inp["L3_ffn_w1_s"])
    d["s2"] = col(inp["L3_ffn_w2_s"])
    d["sh1"] = col(inp["L3_ffn_h1_s"])
    d["sh2"] = col(inp["L3_ffn_h2_s"])
    d["s3"] = col(inp["L3_ffn_w3_s"])
    return d


_CACHE = {}


def get_prog(name, builder):
    if name not in _CACHE:
        _CACHE[name] = builder()
    return _CACHE[name]


CONST3 = ("ones_col", "ones_row")


def run_ffn3(x_fm_shards, prep, consts):
    nc = get_prog("ffn3", build_ffn3)
    in_maps = []
    for c in range(NCORE):
        m = {"x": x_fm_shards[c]}
        m.update(prep)
        feed_consts(m, CONST3, consts)
        in_maps.append(m)
    res = run_bass_kernel_spmd(nc, in_maps, core_ids=list(range(NCORE)))
    return [res.results[c]["y"] for c in range(NCORE)]


def shard_x(x):
    xt = np.ascontiguousarray(x.reshape(B * S // T, T, D).transpose(0, 2, 1))
    return [xt[c] for c in range(NCORE)]


def unshard_x(shards):
    xt = np.stack(shards, axis=0)
    return np.ascontiguousarray(xt.transpose(0, 2, 1)).reshape(B, S, D)


def _fin_preps(inputs):
    return {
        0: {"wo": mblocks(np.eye(D, dtype=np.float32))},
        1: {"wo": mblocks(inputs["L1_attn_Wo"]), "bo": inputs["L1_attn_bo"].reshape(1, D)},
        2: {"wo": mblocks(inputs["L2_attn_wo"])},
        3: {"wo": mblocks(inputs["L3_attn_w_out"].T)},
    }


def kernel(**inputs):
    inputs = {k: np.ascontiguousarray(np.asarray(v, dtype=np.float32)) for k, v in inputs.items()}
    consts = make_consts()
    pf0 = prep_ffn0(inputs)
    pf1 = prep_ffn_moe(1, inputs)
    pf2 = prep_ffn_moe(2, inputs)
    pf3 = prep_ffn3(inputs)
    pa = {L: [prep_attn(L, inputs, 0), prep_attn(L, inputs, 1)] for L in (0, 2, 3)}
    pa1 = [prep_attn1(inputs, 0), prep_attn1(inputs, 1)]
    finp = _fin_preps(inputs)

    shards = shard_x(inputs["L0_attn_x"])

    def attn_block(L, shards):
        x = unshard_x(shards)
        houts = run_attn1(x, pa1, consts) if L == 1 else run_attn(L, x, pa[L], consts)
        return run_fin(L if L != 0 else 0, shards, houts, finp[L], consts)

    shards = run_ffn0(shards, pf0, consts)
    shards = attn_block(0, shards)
    shards = run_ffn0(shards, pf0, consts)
    shards = run_ffn_moe(1, shards, pf1, consts)
    shards = attn_block(1, shards)
    shards = run_ffn_moe(1, shards, pf1, consts)
    shards = run_ffn_moe(2, shards, pf2, consts)
    shards = attn_block(2, shards)
    shards = run_ffn_moe(2, shards, pf2, consts)
    shards = run_ffn3(shards, pf3, consts)
    shards = attn_block(3, shards)
    shards = run_ffn3(shards, pf3, consts)

    return unshard_x(shards)



def build_attn_st(layer, debug=False):
    p = Prog(f"attn{layer}")
    nc = p.nc
    W = S
    nch = W // NC
    sems = {k: p.sem(k) for k in ("pe", "act", "dve", "ld", "st", "cl", "gp")}
    x_d = nc.dram_tensor("x", [D, W], F32, kind="ExternalInput")
    ho_d = nc.dram_tensor("hout", [512, W], BF16, kind="ExternalOutput")
    wq_d = nc.dram_tensor("wq", [4, 128, D], F32, kind="ExternalInput")
    wk_d = nc.dram_tensor("wk", [4, 128, D], F32, kind="ExternalInput")
    wv_d = nc.dram_tensor("wv", [4, 128, D], F32, kind="ExternalInput")
    causal = layer in (2, 3)
    if causal:
        cm_d = nc.dram_tensor("cmask", [128, 4 * NC], F32, kind="ExternalInput")
    has_bias = layer == 0
    if has_bias:
        bq_d = nc.dram_tensor("bq", [1, 512], F32, kind="ExternalInput")
        bk_d = nc.dram_tensor("bk", [1, 512], F32, kind="ExternalInput")
        bv_d = nc.dram_tensor("bv", [1, 512], F32, kind="ExternalInput")
    affine = layer == 2
    if affine:
        lng_d = nc.dram_tensor("lng", [128, FC], F32, kind="ExternalInput")
        lnb_d = nc.dram_tensor("lnb", [128, FC], F32, kind="ExternalInput")

    C = Consts(p, sems["cl"], which=("ones_col", "ones_row", "ones_rowN", "ident_bf", "ident_f32"))
    pp = {"st0": p.psum("pst0", [1, NC]), "st1": p.psum("pst1", [1, NC]),
          "bc0": p.psum("pbc0", [128, NC]), "bc1": p.psum("pbc1", [128, NC]),
          "mm0": p.psum("pmm0", [128, NC]), "mm1": p.psum("pmm1", [128, NC]),
          "av": p.psum("pav", [65, NC]), "tr": p.psum("ptr", [128, 64], BF16)}

    xch = p.sb("xch", [128, FC * NC], F32)
    xn = p.sb("xn", [128, FC * W], BF16)
    qa = p.sb("qa", [128, W], BF16)
    ka = p.sb("ka", [128, W], BF16)
    va = p.sb("va", [128, W], BF16)
    vtm = p.sb("vtm", [128, 32 * 65], BF16)
    pt = [p.sb("pt0", [128, NC], BF16), p.sb("pt1", [128, NC], BF16)]
    wf = p.sb("awf", [128, D], F32)
    wb = p.sb("awb", [128, D], BF16)
    smax = p.sb("smax", [1, 8], F32)
    negm = p.sb("negm", [128, 1], F32)
    tmpc = p.sb("tmpc", [128, NC], F32)
    rrow = p.sb("rrow", [1, NC], F32)
    stage = p.sb("stage", [128, NC], BF16)
    if causal:
        cm_sb = p.sb("cm_sb", [128, 4 * NC], F32)
        p.op("sync", lambda sy: sy.dma_start(out=cm_sb[:, :], in_=cm_d[:, :]),
            inc=sems["ld"], k=16)
    if has_bias:
        bq_sb = p.sb("bq_sb", [1, 512], F32)
        bk_sb = p.sb("bk_sb", [1, 512], F32)
        bv_sb = p.sb("bv_sb", [1, 512], F32)
        p.op("sync", lambda sy: sy.dma_start(out=bq_sb[:, :], in_=bq_d[:, :]), inc=sems["ld"], k=16)
        p.op("sync", lambda sy: sy.dma_start(out=bk_sb[:, :], in_=bk_d[:, :]), inc=sems["ld"], k=16)
        p.op("sync", lambda sy: sy.dma_start(out=bv_sb[:, :], in_=bv_d[:, :]), inc=sems["ld"], k=16)
    aff = None
    if affine:
        lng_sb = p.sb("lng_sb", [128, FC], F32)
        lnb_sb = p.sb("lnb_sb", [128, FC], F32)
        p.op("sync", lambda sy: sy.dma_start(out=lng_sb[:, :], in_=lng_d[:, :]), inc=sems["ld"], k=16)
        p.op("sync", lambda sy: sy.dma_start(out=lnb_sb[:, :], in_=lnb_d[:, :]), inc=sems["ld"], k=16)
        aff = (lng_sb, lnb_sb)
    c_t = sems["ld"].n
    for e in ("vector", "tensor", "scalar"):
        p.wait(e, (sems["cl"], sems["cl"].n), (sems["ld"], c_t))

    for i in range(nch):
        ap3 = x_d.ap()[:, i * NC:(i + 1) * NC].rearrange("(f p) t -> p f t", p=128)
        p.op("sync", lambda sy, a=ap3: sy.dma_start(
            out=fm_sb_ap(xch, NC), in_=a),
            waits=barrier(p, sems, ("pe", "act", "dve")) if i else [], inc=sems["ld"], k=16)
        lt = sems["ld"].n
        for e in ("vector", "tensor", "scalar"):
            p.wait(e, (sems["ld"], lt))
        emit_ln_q(p, f"aln{i}", xch, NC, 0, NC, None, C, pp, sems, affine=aff,
                  xn_out=(xn, i * NC, W))

    scale_q = {0: 1.0 / 2048.0, 2: 0.125, 3: 0.125}[layer]
    scale_k = {0: 1.0 / 32.0, 2: 1.0, 3: 1.0}[layer]
    scale_v = {0: 1.0 / 32.0, 2: 1.0, 3: 1.0}[layer]
    if debug:
        dbg_q = nc.dram_tensor("dbg_q", [128, W], BF16, kind="ExternalOutput")
        dbg_k = nc.dram_tensor("dbg_k", [128, W], BF16, kind="ExternalOutput")
        dbg_v = nc.dram_tensor("dbg_v", [128, W], BF16, kind="ExternalOutput")
        dbg_vtm = nc.dram_tensor("dbg_vtm", [128, 32 * 65], BF16, kind="ExternalOutput")
        dbg_pt = nc.dram_tensor("dbg_pt", [128, NC], BF16, kind="ExternalOutput")
        dbg_neg = nc.dram_tensor("dbg_neg", [128, 1], F32, kind="ExternalOutput")
        dbg_xn = nc.dram_tensor("dbg_xn", [128, FC * W], BF16, kind="ExternalOutput")
        dbg_av = nc.dram_tensor("dbg_av", [128, NC], F32, kind="ExternalOutput")
        dbg_rr = nc.dram_tensor("dbg_rr", [1, NC], F32, kind="ExternalOutput")
        p._dbg = (dbg_q, dbg_k, dbg_v, dbg_vtm, dbg_pt, dbg_neg, dbg_xn, dbg_av, dbg_rr)
    else:
        p._dbg = None
    for ch2 in range(4):
        pre = barrier(p, sems)
        emit_matmul(p, f"pq{ch2}", wq_d, D, 128, xn, W,
                    act_evac(p, lambda mc, i: qa[:, i * NC:(i + 1) * NC],
                             func=AF.Copy, scale=scale_q, sems=sems),
                    sems, pp, wf, wb, pre=pre,
                    wslice=lambda mc, c=ch2: wq_d[c, :, :],
                    bias_row=(lambda mc, c=ch2: bq_sb[0:1, c * 128:(c + 1) * 128]) if has_bias else None,
                    ones_rowN=C["ones_rowN"])
        emit_matmul(p, f"pk{ch2}", wk_d, D, 128, xn, W,
                    act_evac(p, lambda mc, i: ka[:, i * NC:(i + 1) * NC],
                             func=AF.Copy, scale=scale_k, sems=sems),
                    sems, pp, wf, wb, pre=barrier(p, sems),
                    wslice=lambda mc, c=ch2: wk_d[c, :, :],
                    bias_row=(lambda mc, c=ch2: bk_sb[0:1, c * 128:(c + 1) * 128]) if has_bias else None,
                    ones_rowN=C["ones_rowN"])
        emit_matmul(p, f"pv{ch2}", wv_d, D, 128, xn, W,
                    act_evac(p, lambda mc, i: va[:, i * NC:(i + 1) * NC],
                             func=AF.Copy, scale=scale_v, sems=sems),
                    sems, pp, wf, wb, pre=barrier(p, sems),
                    wslice=lambda mc, c=ch2: wv_d[c, :, :],
                    bias_row=(lambda mc, c=ch2: bv_sb[0:1, c * 128:(c + 1) * 128]) if has_bias else None,
                    ones_rowN=C["ones_rowN"])
        proj_done = barrier(p, sems)
        if debug and ch2 == 0:
            p.op("gpsimd", lambda g, dd=p._dbg[6]: g.dma_start(out=dd.ap(), in_=xn[:, :]),
                 waits=proj_done, inc=sems["st"], k=16)
            for dd, bb in ((p._dbg[0], qa), (p._dbg[1], ka), (p._dbg[2], va)):
                p.op("gpsimd", lambda g, dd=dd, bb=bb: g.dma_start(out=dd.ap(), in_=bb[:, :]),
                     waits=proj_done, inc=sems["st"], k=16)
        for sub in range(2):
            hp = 2 * ch2 + sub
            emit_head_st(p, layer, hp, qa, ka, va, vtm, pt, ho_d, stage, C, pp, sems,
                         smax, negm, tmpc, rrow, xch,
                         cm_sb if causal else None, proj_done if sub == 0 else ())

    p.wait("gpsimd", (sems["st"], sems["st"].n))
    return p.build()


def emit_head_st(p, layer, hp, qa, ka, va, vtm, pt, ho_d, stage, C, pp, sems,
                 smax, negm, tmpc, rrow, sqbuf, cm_sb, first_pre):
    W = S
    nch = W // NC
    pe, act, dve, ld = sems["pe"], sems["act"], sems["dve"], sems["ld"]
    base = 64 * (hp % 2)
    causal = layer in (2, 3)
    pre = list(first_pre) + barrier(p, sems, ("pe", "act", "dve"))

    qcur = p.sb_cached("qcur", [65, S], BF16)
    kcur = p.sb_cached("kcur", [65, S], BF16)
    vcur = p.sb_cached("vcur", [64, S], BF16)
    maxb = p.sb_cached("maxb", [128, 8], F32)
    maxc = p.sb_cached("maxc", [128, 1], F32)
    mrow = p.sb_cached("mrow", [1, 128], BF16)
    for dst, srcb in ((qcur, qa), (kcur, ka), (vcur, va)):
        p.op("sync", lambda sy, o=dst[0:64, :], a=srcb[base:base + 64, 0:W]: sy.dma_start(
            out=o, in_=a), waits=pre, inc=ld, k=16)
    cp_t = ld.n
    p.chain("vector", dve, lambda v: v.memset(kcur[64:65, :], 1.0), extra_waits=pre)
    qh = qcur[0:64, :]
    kh = kcur[0:64, :]
    vh = vcur[0:64, :]

    p.chain("vector", dve, lambda v: v.memset(vtm[:, :], 1.0))
    ms_t = dve.n
    for kt in range(32):
        vsl = vh[:, kt * 128:(kt + 1) * 128]
        p.op("tensor", lambda t, a=vsl, ident=C["ident_bf"][0:64, 0:64]:
             t.transpose(pp["tr"][:, 0:64], a, ident),
             waits=[(dve, ms_t), (ld, cp_t)] + ([(act, act.n)] if kt else []), inc=pe)
        p.chain("scalar", act, lambda s, kt=kt: s.activation(
            vtm[:, kt * 65: kt * 65 + 64], pp["tr"][:, 0:64], AF.Copy),
            extra_waits=[(pe, pe.n)])
    vtm_done = act.n
    if getattr(p, "_dbg", None) is not None and hp == 0:
        p.op("gpsimd", lambda g, dd=p._dbg[3]: g.dma_start(out=dd.ap(), in_=vtm[:, :]),
             waits=[(act, vtm_done)], inc=sems["st"], k=16)

    for qt in range(32):
        ktn = min(32, qt + 1) * 128 if causal else W
        nkc = (ktn + NC - 1) // NC
        for kc in range(nkc):
            j = kc % 2
            ps = pp["mm0"] if j == 0 else pp["mm1"]
            lhsT = qh[:, qt * 128:(qt + 1) * 128]
            rhs = kh[:, kc * NC:(kc + 1) * NC]
            p.op("tensor", lambda t, lhsT=lhsT, rhs=rhs, ps=ps: t.matmul(
                ps[:, :], lhsT, rhs, start=True, stop=True),
                waits=[(dve, dve.n), (ld, cp_t)], inc=pe)
            p.chain("vector", dve, lambda v, o=maxb[:, kc:kc + 1], ps=ps:
                    v.reduce_max(o, ps[:, :], axis=AX.X),
                    extra_waits=[(pe, pe.n)])
        p.chain("vector", dve, lambda v, n=nkc: v.reduce_max(
            maxc[:, 0:1], maxb[:, 0:n], axis=AX.X))
        p.chain("vector", dve, lambda v: v.tensor_scalar(
            maxc[:, 0:1], maxc[:, 0:1], -1.0, None, ALU.mult))
        d_t = dve.n
        p.op("tensor", lambda t: t.transpose(
            pp["bc0"][0:1, 0:128], maxc[:, 0:1], C["ident_f32"][0:128, 0:128]),
            waits=[(dve, d_t), (act, act.n), (sems["st"], sems["st"].n)], inc=pe)
        p.chain("scalar", act, lambda s: s.activation(
            mrow[0:1, 0:128], pp["bc0"][0:1, 0:128], AF.Copy),
            extra_waits=[(pe, pe.n)])
        p.op("gpsimd", lambda g, qt=qt: g.dma_start(
            out=qcur[64:65, qt * 128:(qt + 1) * 128], in_=mrow[0:1, 0:128]),
            waits=[(sems["act"], act.n)], inc=sems["st"], k=16)
    spass_done = barrier(p, sems) + [(sems["st"], sems["st"].n)]

    av_free = []
    stage_free = []
    ps_free = [[], []]
    pt_free = [[], []]
    for qb in range(nch):
        ktmax = (4 * qb + 4) if causal else 32
        for kt in range(ktmax):
            j = kt % 2
            ps = pp["mm0"] if j == 0 else pp["mm1"]
            lhsT = kcur[0:65, kt * 128:(kt + 1) * 128]
            rhs = qcur[0:65, qb * NC:(qb + 1) * NC]
            p.op("tensor", lambda t, lhsT=lhsT, rhs=rhs, ps=ps: t.matmul(
                ps[:, :], lhsT, rhs, start=True, stop=True),
                waits=spass_done + ps_free[j], inc=pe)
            mm_t = pe.n
            dcm = kt - 4 * qb
            if causal and 0 <= dcm < 4:
                p.chain("vector", dve, lambda v, ps=ps, m=cm_sb[:, dcm * NC:(dcm + 1) * NC]:
                        v.tensor_add(ps[:, :], ps[:, :], m),
                        extra_waits=[(pe, mm_t)])
                exp_w = [(dve, dve.n)]
            else:
                exp_w = [(pe, mm_t)]
            p.chain("scalar", act, lambda s, o=pt[j], ps=ps: s.activation(
                o[:, :], ps[:, :], AF.Exp),
                extra_waits=exp_w + pt_free[j])
            e_t = act.n
            if getattr(p, "_dbg", None) is not None and hp == 0 and qb == 0 and kt == 0:
                p.op("gpsimd", lambda g, dd=p._dbg[4], o=pt[j]: g.dma_start(out=dd.ap(), in_=o[:, :]),
                     waits=[(act, e_t)], inc=sems["st"], k=16)
                e_t_extra = [(sems["st"], sems["st"].n)]
            else:
                e_t_extra = []
            ps_free[j] = [(act, e_t)] + e_t_extra
            p.op("tensor", lambda t, lhsT=vtm[:, kt * 65:(kt + 1) * 65], rhs=pt[j],
                 f0=(kt == 0), f1=(kt == ktmax - 1): t.matmul(
                     pp["av"][:, :], lhsT, rhs[:, :], start=f0, stop=f1),
                 waits=[(act, e_t)] + ([(act, vtm_done)] if kt == 0 else []) + (av_free if kt == 0 else []),
                 inc=pe)
            pt_free[j] = [(pe, pe.n)]
        av_t = pe.n
        p.chain("vector", dve, lambda v: v.reciprocal(rrow[0:1, :], pp["av"][64:65, :]),
                extra_waits=[(pe, av_t)])
        r_t = dve.n
        p.op("tensor", lambda t, orr=C["ones_row"]: t.matmul(
            pp["bc1"][0:64, :], orr[0:1, 0:64], rrow[0:1, :], start=True, stop=True),
            waits=[(dve, r_t)], inc=pe)
        b_t = pe.n
        p.chain("scalar", act, lambda s: s.activation(tmpc[0:64, :], pp["av"][0:64, :], AF.Copy),
                extra_waits=[(pe, b_t)])
        a2 = act.n
        if getattr(p, "_dbg", None) is not None and hp == 0 and qb == 0:
            p.op("gpsimd", lambda g, dd=p._dbg[7]: g.dma_start(out=dd.ap()[0:64, :], in_=tmpc[0:64, :]),
                 waits=[(act, a2)], inc=sems["st"], k=16)
            p.op("gpsimd", lambda g, dd=p._dbg[8]: g.dma_start(out=dd.ap(), in_=rrow[0:1, :]),
                 inc=sems["st"], k=16)
        p.chain("vector", dve, lambda v, o=stage[0:64, :]: v.tensor_mul(
            o, tmpc[0:64, :], pp["bc1"][0:64, :]),
            extra_waits=[(act, a2)] + stage_free)
        m_t = dve.n
        av_free = [(act, a2), (dve, r_t)]
        dst = ho_d.ap()[hp * 64:(hp + 1) * 64, qb * NC:(qb + 1) * NC]
        p.op("gpsimd", lambda g, o=dst: g.dma_start(out=o, in_=stage[0:64, :]),
             waits=[(dve, m_t)], inc=sems["st"], k=16)
        stage_free = [(sems["st"], sems["st"].n)]


def build_fin(layer):
    p = Prog(f"fin{layer}")
    nc = p.nc
    sems = {k: p.sem(k) for k in ("pe", "act", "dve", "ld", "st", "cl", "gp")}
    pp = std_pp(p)
    x_d = nc.dram_tensor("x", [D, T], F32, kind="ExternalInput")
    y_d = nc.dram_tensor("y", [D, T], F32, kind="ExternalOutput")
    h_d = nc.dram_tensor("head", [D, T], BF16, kind="ExternalInput")
    wo_d = nc.dram_tensor("wo", [8, 128, D], F32, kind="ExternalInput")
    has_bias = layer == 1
    if has_bias:
        bo_d = nc.dram_tensor("bo", [1, D], F32, kind="ExternalInput")
    C = Consts(p, sems["cl"], which=("ones_col", "ones_row", "ones_rowN"))

    x_sb = p.sb("x_sb", [128, FC * T], F32)
    h_sb = p.sb("h_sb", [128, FC * T], BF16)
    wf = p.sb("fwf", [128, D], F32)
    wb = p.sb("fwb", [128, D], BF16)
    tmp = p.sb("ftmp", [128, NC], F32)
    p.op("sync", lambda sy: sy.dma_start(out=fm_sb_ap(x_sb, T), in_=fm_dram_ap(x_d)),
         inc=sems["ld"], k=16)
    p.op("sync", lambda sy: sy.dma_start(out=fm_sb_ap(h_sb, T), in_=fm_dram_ap(h_d)),
         inc=sems["ld"], k=16)
    if has_bias:
        bo_sb = p.sb("bo_sb", [1, D], F32)
        p.op("sync", lambda sy: sy.dma_start(out=bo_sb[0:1, :], in_=bo_d[0:1, :]),
             inc=sems["ld"], k=16)
    ld_t = sems["ld"].n
    for e in ("vector", "tensor", "scalar"):
        p.wait(e, (sems["ld"], ld_t), (sems["cl"], sems["cl"].n))

    if layer == 2:
        lnf = p.sb("lnf32", [128, FC * T], F32)
        emit_ln_q(p, "fln", x_sb, T, 0, T, lnf, C, pp, sems)
        d0 = sems["dve"].n
        for fc in range(FC):
            for i in range(T // NC):
                a = fm_view(x_sb, fc, T, i * NC, NC)
                bq = fm_view(lnf, fc, T, i * NC, NC)
                p.chain("vector", sems["dve"], lambda v, a=a, b=bq: v.tensor_add(a, a, b),
                        extra_waits=[(sems["dve"], d0)])

    def cb(mc, i, ps, waits):
        p.chain("scalar", sems["act"],
                lambda s, o=tmp, ps=ps: s.activation(o[:, 0:NC], ps[:, :], AF.Copy),
                extra_waits=waits)
        a_t = sems["act"].n
        xa = fm_view(x_sb, mc, T, i * NC, NC)
        p.chain("vector", sems["dve"],
                lambda v, xa=xa, o=tmp: v.tensor_add(xa, xa, o[:, 0:NC]),
                extra_waits=[(sems["act"], a_t)])
        return [(sems["dve"], sems["dve"].n)]

    emit_matmul(p, "wo", wo_d, D, D, h_sb, T, cb, sems, pp, wf, wb,
                pre=barrier(p, sems),
                bias_row=(lambda mc: bo_sb[0:1, mc * 128:(mc + 1) * 128]) if has_bias else None,
                ones_rowN=C["ones_rowN"])
    finish_frame(p, x_sb, y_d, T, sems, barrier(p, sems))
    return p.build()


def prep_attn(layer, inp, half):
    d = {}
    if layer == 3:
        wqkv = inp["L3_attn_w_qkv"]
        for nm, off in (("wq", 0), ("wk", 1024), ("wv", 2048)):
            sl = wqkv[off + 512 * half: off + 512 * half + 512]
            d[nm] = mblocks(sl.T)
        dlt = np.arange(4)[:, None, None] * 128 + np.arange(128)[None, :, None]
        qq = np.arange(NC)[None, None, :]
        cm = np.where(dlt > qq, np.float32(-10000.0), np.float32(0.0)).astype(np.float32)
        d["cmask"] = np.ascontiguousarray(cm.transpose(1, 0, 2).reshape(128, 4 * NC))
    elif layer == 0:
        Wqk = inp["L0_attn_Wqk"]
        heads = range(8 * half, 8 * half + 8)
        qcols = np.concatenate([Wqk[:, 128 * h: 128 * h + 64] for h in heads], axis=1)
        kcols = np.concatenate([Wqk[:, 128 * h + 64: 128 * h + 128] for h in heads], axis=1)
        vcols = inp["L0_attn_Wv"][:, 512 * half: 512 * half + 512]
        d["wq"] = mblocks(qcols)
        d["wk"] = mblocks(kcols)
        d["wv"] = mblocks(vcols)
        bqk = inp["L0_attn_bqk"]
        d["bq"] = np.concatenate([bqk[128 * h: 128 * h + 64] for h in heads]).reshape(1, 512)
        d["bk"] = np.concatenate([bqk[128 * h + 64: 128 * h + 128] for h in heads]).reshape(1, 512)
        d["bv"] = inp["L0_attn_bv"][512 * half: 512 * half + 512].reshape(1, 512)
    elif layer == 2:
        heads = range(8 * half, 8 * half + 8)
        d["wq"] = mblocks(np.concatenate([inp["L2_attn_wq"][h] for h in heads], axis=1))
        d["wk"] = mblocks(np.concatenate([inp["L2_attn_wk"][h] for h in heads], axis=1))
        d["wv"] = mblocks(np.concatenate([inp["L2_attn_wv"][h] for h in heads], axis=1))
        dlt = np.arange(4)[:, None, None] * 128 + np.arange(128)[None, :, None]
        qq = np.arange(NC)[None, None, :]
        cm = np.where(dlt > qq, np.float32(-1e30), np.float32(0.0)).astype(np.float32)
        d["cmask"] = np.ascontiguousarray(cm.transpose(1, 0, 2).reshape(128, 4 * NC))
        d["lng"] = np.ascontiguousarray(inp["L2_attn_ln_w"].reshape(FC, 128).T)
        d["lnb"] = np.ascontiguousarray(inp["L2_attn_ln_b"].reshape(FC, 128).T)
    return d


def run_attn(layer, x, preps, consts):
    nc = get_prog(f"attn{layer}", lambda: build_attn_st(layer))
    xfm = [np.ascontiguousarray(x[b].T) for b in range(B)]
    names = ("ones_col", "ones_row", "ones_rowN", "ident_bf", "ident_f32")
    in_maps = []
    for c in range(NCORE):
        b, half = c // 2, c % 2
        m = {"x": xfm[b]}
        m.update(preps[half])
        feed_consts(m, names, consts)
        in_maps.append(m)
    res = run_bass_kernel_spmd(nc, in_maps, core_ids=list(range(NCORE)))
    return [res.results[c]["hout"] for c in range(NCORE)]


def stitch_heads(houts, c):
    b, half = c // 2, c % 2
    lo = houts[2 * b][:, half * T:(half + 1) * T]
    hi = houts[2 * b + 1][:, half * T:(half + 1) * T]
    return np.ascontiguousarray(np.concatenate([lo, hi], axis=0))


def run_fin(layer, x_shards, houts, prep, consts):
    nc = get_prog(f"fin{layer}", lambda: build_fin(layer))
    names = ("ones_col", "ones_row", "ones_rowN")
    in_maps = []
    for c in range(NCORE):
        m = {"x": x_shards[c], "head": stitch_heads(houts, c)}
        m.update(prep)
        feed_consts(m, names, consts)
        in_maps.append(m)
    res = run_bass_kernel_spmd(nc, in_maps, core_ids=list(range(NCORE)))
    return [res.results[c]["y"] for c in range(NCORE)]


def dram_tmp(nc, name, shape, dt=F32):
    return nc.dram_tensor(name, list(shape), dt, kind="Internal")



CAP = 1024
PW = 512


def emit_router(p, tag, xn, rw_sb, C, pp, sems, strict_all, want_probs):
    pe, act, dve, ld = sems["pe"], sems["act"], sems["dve"], sems["ld"]
    op = strict_all
    NT = T // 128
    lg_tm = p.sb(f"{tag}_lg", [128, NT * 8], F32)
    oh_bf = p.sb(f"{tag}_oh", [128, NT * 8], BF16)
    oh_f = p.sb(f"{tag}_ohf", [128, 8], F32)
    mx = p.sb(f"{tag}_mx", [128, 1], F32)
    ecol = p.sb(f"{tag}_e", [128, NT], F32)
    gcol = p.sb(f"{tag}_g", [128, NT], F32)
    slot = p.sb(f"{tag}_slot", [128, NT], I32)
    pcol = p.sb(f"{tag}_p", [128, NT], F32)
    tot = p.sb(f"{tag}_tot", [1, 8], F32)
    scr = p.sb(f"{tag}_scr", [128, 8], F32)

    rw_bf = p.sb(f"{tag}_rwbf", [128, FC * 8], BF16)
    op("vector", lambda v: v.tensor_copy(rw_bf[:, :], rw_sb[:, :]), inc=dve)
    for tt in range(NT):
        for kc in range(FC):
            lhsT = xn[:, kc * T + tt * 128: kc * T + (tt + 1) * 128]
            rhs = rw_bf[:, kc * 8:(kc + 1) * 8]
            op("tensor", lambda t, lhsT=lhsT, rhs=rhs, f0=(kc == 0), f1=(kc == FC - 1):
               t.matmul(pp["bc0"][:, 0:8], lhsT, rhs, start=f0, stop=f1), inc=pe)
        lg = lg_tm[:, tt * 8:(tt + 1) * 8]
        op("scalar", lambda s, o=lg: s.activation(o, pp["bc0"][:, 0:8], AF.Copy), inc=act)
    op("vector", lambda v: v.memset(tot[0:1, :], 0.0), inc=dve)
    for tt in range(NT):
        lg = lg_tm[:, tt * 8:(tt + 1) * 8]
        ohb = oh_bf[:, tt * 8:(tt + 1) * 8]
        ec = ecol[:, tt:tt + 1]
        gc = gcol[:, tt:tt + 1]
        op("vector", lambda v, lg=lg: v.reduce_max(mx[:, 0:1], lg, axis=AX.X), inc=dve)
        op("vector", lambda v, lg=lg: v.tensor_scalar(oh_f[:, :], lg, mx[:, 0:1], None, ALU.is_ge), inc=dve)
        op("vector", lambda v, ohb=ohb: v.tensor_copy(ohb, oh_f[:, :]), inc=dve)
        op("vector", lambda v: v.tensor_mul(scr[:, :], oh_f[:, :], C["iota8"][:, :]), inc=dve)
        op("vector", lambda v, ec=ec: v.reduce_sum(ec, scr[:, :], axis=AX.X), inc=dve)
        if want_probs:
            op("vector", lambda v, lg=lg: v.tensor_scalar(mx[:, 0:1], mx[:, 0:1], -1.0, None, ALU.mult), inc=dve)
            pc = pcol[:, tt:tt + 1]
            op("scalar", lambda s, lg=lg: s.activation(scr[:, :], lg, AF.Exp,
                                                       bias=mx[:, 0:1]), inc=act)
            op("vector", lambda v: v.reduce_sum(mx[:, 0:1], scr[:, :], axis=AX.X), inc=dve)
            op("vector", lambda v, pc=pc: v.reciprocal(pc, mx[:, 0:1]), inc=dve)
        op("tensor", lambda t, ohb=ohb: t.matmul(pp["bc1"][:, 0:8], C["lt_strict"][:, :],
                                                 ohb, start=True, stop=False), inc=pe)
        op("tensor", lambda t: t.matmul(pp["bc1"][:, 0:8], C["ones_row"][0:1, :],
                                        tot[0:1, :], start=False, stop=True), inc=pe)
        op("vector", lambda v: v.tensor_mul(scr[:, :], oh_f[:, :], pp["bc1"][:, 0:8]), inc=dve)
        op("vector", lambda v, gc=gc: v.reduce_sum(gc, scr[:, :], axis=AX.X), inc=dve)
        op("tensor", lambda t: t.matmul(pp["st0"][0:1, 0:8], C["ones_col"][:, 0:1],
                                        oh_f[:, :], start=True, stop=True), inc=pe)
        op("vector", lambda v: v.tensor_add(tot[0:1, :], tot[0:1, :], pp["st0"][0:1, 0:8]), inc=dve)
        op("vector", lambda v, ec=ec: v.tensor_scalar(scr[:, 0:1], ec, float(CAP), None, ALU.mult), inc=dve)
        op("vector", lambda v, gc=gc: v.tensor_add(scr[:, 0:1], scr[:, 0:1], gc), inc=dve)
        op("vector", lambda v, sl=slot[:, tt:tt + 1]: v.tensor_copy(sl, scr[:, 0:1]), inc=dve)
    return {"oh": oh_bf, "e": ecol, "g": gcol, "slot": slot, "p": pcol, "tot": tot}


def emit_tm_transpose(p, src_bf, s_base, s_stride, dst_bf, d_base, d_stride,
                      nchunks, C, pp, sems, op):
    pe, act = sems["pe"], sems["act"]
    for b_ in range(nchunks):
        sl = src_bf[:, s_base + b_ * s_stride: s_base + b_ * s_stride + 128]
        dl = dst_bf[:, d_base + b_ * d_stride: d_base + b_ * d_stride + 128]
        op("tensor", lambda t, sl=sl: t.transpose(pp["tr"][:, 0:128], sl,
                                                  C["ident_bf"][:, :]), inc=pe)
        op("scalar", lambda s, dl=dl: s.activation(dl, pp["tr"][:, 0:128], AF.Copy), inc=act)


def build_ffn_moe(layer):
    assert layer in (1, 2)
    p = Prog(f"ffn{layer}")
    nc = p.nc
    sems = {k: p.sem(k) for k in ("pe", "act", "dve", "ld", "st", "cl", "gp", "cc")}
    pp = std_pp(p)
    pp["tr"] = p.psum("ptr", [128, 128], BF16)
    x_d = nc.dram_tensor("x", [D, T], F32, kind="ExternalInput")
    y_d = nc.dram_tensor("y", [D, T], F32, kind="ExternalOutput")
    rw_d = nc.dram_tensor("rw", [128, FC * 8], F32, kind="ExternalInput")
    DH = 4096 if layer == 1 else 2816
    DHC = DH // 128
    if layer == 1:
        w1_d = nc.dram_tensor("w1", [8, DHC, 128, D], F32, kind="ExternalInput")
        w2_d = nc.dram_tensor("w2", [8, FC, 128, DH], F32, kind="ExternalInput")
        b1_d = nc.dram_tensor("b1", [8, DH], F32, kind="ExternalInput")
        b2_d = nc.dram_tensor("b2", [8, D], F32, kind="ExternalInput")
        rmask_d = nc.dram_tensor("rmask", [8, 1], F32, kind="ExternalInput")
        cc_in = dram_tmp(nc, "cc_in", [1, 8], F32)
        cc_out = dram_tmp(nc, "cc_out", [8, 8], F32)
    else:
        gw_d = nc.dram_tensor("gw", [8, DHC, 128, D], F32, kind="ExternalInput")
        uw_d = nc.dram_tensor("uw", [8, DHC, 128, D], F32, kind="ExternalInput")
        dw_d = nc.dram_tensor("dw", [8, FC, 128, DH], F32, kind="ExternalInput")
    buf_dram = dram_tmp(nc, "bufrows", [8 * CAP, D], BF16)
    eo_dram = dram_tmp(nc, "eorows", [8 * CAP, D], BF16)

    C = Consts(p, sems["cl"], which=("ones_col", "ones_row", "ones_rowN", "ident_bf",
                                     "iota8", "lt_strict"))
    allsems = [sems[k] for k in ("pe", "act", "dve", "ld", "st", "gp", "cc")]

    def op(eng, emit, waits=(), inc=None, k=None):
        kk = k if k is not None else (16 if inc in (sems["ld"], sems["st"], sems["gp"]) else 1)
        p.op(eng, emit, waits=list(waits) + [(s, s.n) for s in allsems], inc=inc, k=kk)

    x_sb = p.sb("x_sb", [128, FC * T], F32)
    xn = p.sb("xn", [128, FC * T], BF16)
    rw_sb = p.sb("rw_sb", [128, FC * 8], F32)
    p.op("sync", lambda sy: sy.dma_start(out=fm_sb_ap(x_sb, T), in_=fm_dram_ap(x_d)),
         inc=sems["ld"], k=16)
    p.op("sync", lambda sy: sy.dma_start(out=rw_sb[:, :], in_=rw_d[:, :]), inc=sems["ld"], k=16)
    ld_t = sems["ld"].n
    for e in ("vector", "tensor", "scalar", "gpsimd"):
        p.wait(e, (sems["ld"], ld_t), (sems["cl"], sems["cl"].n))

    emit_ln_q(p, "ln", x_sb, T, 0, T, xn, C, pp, sems)

    rt = emit_router(p, "rt", xn, rw_sb, C, pp, sems, op, want_probs=(layer == 1))

    xn_tm = p.sb("xn_tm", [128, D], BF16)
    NT = T // 128
    for tt in range(NT):
        emit_tm_transpose(p, xn, tt * 128, T, xn_tm, 0, 128, FC, C, pp, sems, op)
        op("gpsimd", lambda g, tt=tt: g.indirect_dma_start(
            buf_dram.ap(), IndirectOffsetOnAxis(rt["slot"][:, tt:tt + 1], 0),
            xn_tm[:, :], None), inc=sems["gp"])

    if layer == 1:
        rm_sb = p.sb("rm_sb", [8, 1], F32)
        cnt_sb = p.sb("cnt_sb", [8, 8], BF16)
        offs = p.sb("offs", [1, 8], F32)
        gate = p.sb("gate", [128, NT], F32)
        scr2 = p.sb("scr2", [128, 8], F32)
        p.op("sync", lambda sy: sy.dma_start(out=rm_sb[:, :], in_=rmask_d[:, :]),
             inc=sems["ld"], k=16)
        op("gpsimd", lambda g: g.dma_start(out=cc_in.ap(), in_=rt["tot"][0:1, :]),
           inc=sems["gp"])
        op("gpsimd", lambda g: g.collective_compute(
            "AllGather", ALU.bypass, replica_groups=[list(range(8))],
            ins=[cc_in.ap().opt()], outs=[cc_out.ap().opt()]), inc=sems["cc"], k=1)
        cnt_f = p.sb("cnt_f", [8, 8], F32)
        op("gpsimd", lambda g: g.dma_start(out=cnt_f[:, :], in_=cc_out.ap()), inc=sems["gp"])
        op("vector", lambda v: v.tensor_copy(cnt_sb[:, :], cnt_f[:, :]), inc=sems["dve"])
        rm_bf = p.sb("rm_bf", [8, 1], BF16)
        op("vector", lambda v: v.tensor_copy(rm_bf[:, :], rm_sb[:, :]), inc=sems["dve"])
        op("tensor", lambda t: t.matmul(pp["st0"][0:1, 0:8], rm_bf[0:8, 0:1],
                                        cnt_sb[0:8, :], start=True, stop=True), inc=sems["pe"])
        op("scalar", lambda s: s.activation(offs[0:1, :], pp["st0"][0:1, 0:8], AF.Copy),
           inc=sems["act"])
        for tt in range(NT):
            ohb = rt["oh"][:, tt * 8:(tt + 1) * 8]
            op("tensor", lambda t: t.matmul(pp["bc1"][:, 0:8], C["ones_row"][0:1, :],
                                            offs[0:1, :], start=True, stop=True), inc=sems["pe"])
            op("vector", lambda v, ohb=ohb: v.tensor_copy(scr2[:, :], ohb), inc=sems["dve"])
            op("vector", lambda v: v.tensor_mul(scr2[:, :], scr2[:, :], pp["bc1"][:, 0:8]),
               inc=sems["dve"])
            op("vector", lambda v: v.reduce_sum(scr2[:, 0:1], scr2[:, :], axis=AX.X),
               inc=sems["dve"])
            op("vector", lambda v, g_=rt["g"][:, tt:tt + 1]: v.tensor_add(
                scr2[:, 0:1], scr2[:, 0:1], g_), inc=sems["dve"])
            op("vector", lambda v: v.tensor_scalar(scr2[:, 1:2], scr2[:, 0:1], 3072.0,
                                                   None, ALU.is_lt), inc=sems["dve"])
            op("vector", lambda v, pc=rt["p"][:, tt:tt + 1], gt=gate[:, tt:tt + 1]:
               v.tensor_mul(gt, pc, scr2[:, 1:2]), inc=sems["dve"])
            op("vector", lambda v, gt=gate[:, tt:tt + 1]: v.tensor_scalar(
                gt, gt, 0.5, None, ALU.mult), inc=sems["dve"])

    be_tm = p.sb("be_tm", [128, D], BF16)
    be_fm = p.sb("be_fm", [128, FC * PW], BF16)
    if layer == 1:
        h_fm = xn
        h2_fm = None
    else:
        h_fm = p.sb("h_fm", [128, DHC * PW], BF16)
        h2_fm = xn
    eo_fm = p.sb("eo_fm", [128, FC * PW], BF16)
    wf = p.sb("mwf", [128, DH], F32)
    wb = p.sb("mwb", [128, DH], BF16)
    brow = p.sb("brow", [1, DH], F32) if layer == 1 else None
    brow2 = p.sb("brow2", [1, D], F32) if layer == 1 else None

    for e in range(8):
        if layer == 1:
            op("sync", lambda sy, e=e: sy.dma_start(out=brow[0:1, :], in_=b1_d[e:e + 1, :]),
               inc=sems["ld"])
            op("sync", lambda sy, e=e: sy.dma_start(out=brow2[0:1, :], in_=b2_d[e:e + 1, :]),
               inc=sems["ld"])
        for hh in range(CAP // PW):
            base = e * CAP + hh * PW
            for i in range(PW // 128):
                op("sync", lambda sy, base=base, i=i: sy.dma_start(
                    out=be_tm[:, :], in_=buf_dram.ap()[base + i * 128: base + (i + 1) * 128, :]),
                   inc=sems["ld"])
                emit_tm_transpose(p, be_tm, 0, 128, be_fm, i * 128, PW, FC, C, pp, sems, op)
            if layer == 1:
                emit_matmul(p, f"m1_{e}_{hh}", w1_d, D, DH, be_fm, PW,
                            act_evac(p, lambda mc, i: fm_view(h_fm, mc, PW, i * NC, NC),
                                     func=AF.Gelu, sems=sems),
                            sems, pp, wf, wb, pre=barrier(p, sems) + [(sems["ld"], sems["ld"].n)],
                            wslice=lambda mc, e=e: w1_d[e, mc, :, :],
                            bias_row=lambda mc: brow[0:1, mc * 128:(mc + 1) * 128],
                            ones_rowN=C["ones_rowN"])
                def cb2(mc, i, ps, waits):
                    op("scalar", lambda s, o=fm_view(eo_fm, mc, PW, i * NC, NC), ps=ps:
                       s.activation(o, ps[:, :], AF.Copy), waits=waits, inc=sems["act"])
                    return [(sems["act"], sems["act"].n)]
                emit_matmul(p, f"m2_{e}_{hh}", w2_d, DH, D, h_fm, PW, cb2, sems, pp, wf, wb,
                            pre=barrier(p, sems),
                            wslice=lambda mc, e=e: w2_d[e, mc, :, :],
                            bias_row=lambda mc: brow2[0:1, mc * 128:(mc + 1) * 128],
                            ones_rowN=C["ones_rowN"])
            else:
                emit_matmul(p, f"g_{e}_{hh}", gw_d, D, DH, be_fm, PW,
                            act_evac(p, lambda mc, i: fm_view(h_fm, mc, PW, i * NC, NC),
                                     func=AF.Silu, sems=sems),
                            sems, pp, wf, wb, pre=barrier(p, sems),
                            wslice=lambda mc, e=e: gw_d[e, mc, :, :])
                emit_matmul(p, f"u_{e}_{hh}", uw_d, D, DH, be_fm, PW,
                            act_evac(p, lambda mc, i: fm_view(h2_fm, mc, PW, i * NC, NC),
                                     func=AF.Copy, sems=sems),
                            sems, pp, wf, wb, pre=barrier(p, sems),
                            wslice=lambda mc, e=e: uw_d[e, mc, :, :])
                for mc in range(DHC):
                    op("vector", lambda v, a=fm_view(h_fm, mc, PW, 0, PW),
                       b=fm_view(h2_fm, mc, PW, 0, PW): v.tensor_mul(a, a, b), inc=sems["dve"])
                def cb2(mc, i, ps, waits):
                    op("scalar", lambda s, o=fm_view(eo_fm, mc, PW, i * NC, NC), ps=ps:
                       s.activation(o, ps[:, :], AF.Copy), waits=waits, inc=sems["act"])
                    return [(sems["act"], sems["act"].n)]
                emit_matmul(p, f"d_{e}_{hh}", dw_d, DH, D, h_fm, PW, cb2, sems, pp, wf, wb,
                            pre=barrier(p, sems),
                            wslice=lambda mc, e=e: dw_d[e, mc, :, :])
            for i in range(PW // 128):
                emit_tm_transpose(p, eo_fm, i * 128, PW, be_tm, 0, 128, FC, C, pp, sems, op)
                op("gpsimd", lambda g, base=base, i=i: g.dma_start(
                    out=eo_dram.ap()[base + i * 128: base + (i + 1) * 128, :],
                    in_=be_tm[:, :]), inc=sems["gp"])

    out_tm = p.sb("out_tm", [128, D], BF16)
    tmpa = p.sb("tmpa", [128, 128], F32)
    for tt in range(NT):
        op("gpsimd", lambda g, tt=tt: g.indirect_dma_start(
            out_tm[:, :], None, eo_dram.ap(),
            IndirectOffsetOnAxis(rt["slot"][:, tt:tt + 1], 0)), inc=sems["gp"])
        if layer == 1:
            op("vector", lambda v, gt=None, tt=tt: v.tensor_scalar(
                out_tm[:, :], out_tm[:, :],
                p._gate[:, tt:tt + 1] if False else gate[:, tt:tt + 1], None, ALU.mult),
               inc=sems["dve"])
        for fc in range(FC):
            sl = out_tm[:, fc * 128:(fc + 1) * 128]
            op("tensor", lambda t, sl=sl: t.transpose(pp["tr"][:, 0:128], sl,
                                                      C["ident_bf"][:, :]), inc=sems["pe"])
            sc = 1.0 if layer == 1 else 0.5
            op("scalar", lambda s, sc=sc: s.activation(tmpa[:, :], pp["tr"][:, 0:128],
                                                       AF.Copy, scale=sc), inc=sems["act"])
            xa = x_sb[:, fc * T + tt * 128: fc * T + (tt + 1) * 128]
            op("vector", lambda v, xa=xa: v.tensor_add(xa, xa, tmpa[:, :]), inc=sems["dve"])

    finish_frame(p, x_sb, y_d, T, sems, barrier(p, sems))
    return p.build()


def prep_ffn_moe(layer, inp):
    d = {}
    if layer == 1:
        rw = inp["L1_ffn_router_w"]
        d["rw"] = np.ascontiguousarray(rw.T.reshape(FC, 128, 8).transpose(1, 0, 2).reshape(128, FC * 8))
        d["w1"] = np.stack([mblocks(inp["L1_ffn_w1"][e]) for e in range(8)])
        d["w2"] = np.stack([mblocks(inp["L1_ffn_w2"][e]) for e in range(8)])
        d["b1"] = np.ascontiguousarray(inp["L1_ffn_b1"])
        d["b2"] = np.ascontiguousarray(inp["L1_ffn_b2"])
    else:
        rw = inp["L2_ffn_router_w"]
        d["rw"] = np.ascontiguousarray(rw.T.reshape(FC, 128, 8).transpose(1, 0, 2).reshape(128, FC * 8))
        d["gw"] = np.stack([mblocks(inp["L2_ffn_gate_w"][e].T) for e in range(8)])
        d["uw"] = np.stack([mblocks(inp["L2_ffn_up_w"][e].T) for e in range(8)])
        d["dw"] = np.stack([mblocks(inp["L2_ffn_down_w"][e].T) for e in range(8)])
    return d


MOE_CONSTS = ("ones_col", "ones_row", "ones_rowN", "ident_bf", "iota8", "lt_strict")


def run_ffn_moe(layer, x_shards, prep, consts):
    nc = get_prog(f"ffnmoe{layer}", lambda: build_ffn_moe(layer))
    in_maps = []
    for c in range(NCORE):
        m = {"x": x_shards[c]}
        m.update(prep)
        if layer == 1:
            m["rmask"] = (np.arange(8) < c).astype(np.float32).reshape(8, 1)
        feed_consts(m, MOE_CONSTS, consts)
        in_maps.append(m)
    res = run_bass_kernel_spmd(nc, in_maps, core_ids=list(range(NCORE)))
    return [res.results[c]["y"] for c in range(NCORE)]



def build_ffn0():
    p = Prog("ffn0")
    nc = p.nc
    sems = {k: p.sem(k) for k in ("pe", "act", "dve", "ld", "st", "cl", "gp", "cc")}
    pp = std_pp(p)
    x_d = nc.dram_tensor("x", [D, T], F32, kind="ExternalInput")
    y_d = nc.dram_tensor("y", [D, T], F32, kind="ExternalOutput")
    rw_d = nc.dram_tensor("rw", [128, FC * 8], F32, kind="ExternalInput")
    br_d = nc.dram_tensor("br", [1, 8], F32, kind="ExternalInput")
    w1_d = nc.dram_tensor("w1", [8 * FC * 128, D], F32, kind="ExternalInput")
    w2_d = nc.dram_tensor("w2", [8 * FC * 128, D], F32, kind="ExternalInput")
    b1_d = nc.dram_tensor("b1", [8, D], F32, kind="ExternalInput")
    b2_d = nc.dram_tensor("b2", [8, D], F32, kind="ExternalInput")
    cc_in = dram_tmp(nc, "cc_in", [FC, 128], BF16)
    cc_out = dram_tmp(nc, "cc_out", [2 * FC, 128], BF16)

    C = Consts(p, sems["cl"], which=("ones_col", "ones_row", "ones_rowN", "ident_bf",
                                     "ident_f32", "iota8", "iota128"))
    allsems = [sems[k] for k in ("pe", "act", "dve", "ld", "st", "gp", "cc")]

    def op(eng, emit, waits=(), inc=None, k=None):
        kk = k if k is not None else (16 if inc in (sems["ld"], sems["st"], sems["gp"]) else 1)
        p.op(eng, emit, waits=list(waits) + [(s, s.n) for s in allsems], inc=inc, k=kk)

    x_sb = p.sb("x_sb", [128, FC * T], F32)
    xn = p.sb("xn", [128, FC * T], BF16)
    rw_sb = p.sb("rw_sb", [128, FC * 8], F32)
    rw_bf = p.sb("rw_bf", [128, FC * 8], BF16)
    br_sb = p.sb("br_sb", [1, 8], F32)
    p.op("sync", lambda sy: sy.dma_start(out=fm_sb_ap(x_sb, T), in_=fm_dram_ap(x_d)),
         inc=sems["ld"], k=16)
    p.op("sync", lambda sy: sy.dma_start(out=rw_sb[:, :], in_=rw_d[:, :]), inc=sems["ld"], k=16)
    p.op("sync", lambda sy: sy.dma_start(out=br_sb[:, :], in_=br_d[:, :]), inc=sems["ld"], k=16)
    ld_t = sems["ld"].n
    for e in ("vector", "tensor", "scalar", "gpsimd"):
        p.wait(e, (sems["ld"], ld_t), (sems["cl"], sems["cl"].n))

    emit_ln_q(p, "ln", x_sb, T, 0, T, xn, C, pp, sems)

    op("vector", lambda v: v.tensor_copy(rw_bf[:, :], rw_sb[:, :]), inc=sems["dve"])
    for fc in range(FC):
        op("gpsimd", lambda g, fc=fc: g.dma_start(
            out=cc_in.ap()[fc:fc + 1, :].rearrange("o p -> p o"),
            in_=xn[:, fc * T: fc * T + 1], allow_slow_non_contiguous=True), inc=sems["gp"])
    op("gpsimd", lambda g: g.collective_compute(
        "AllGather", ALU.bypass, replica_groups=[[0, 1], [2, 3], [4, 5], [6, 7]],
        ins=[cc_in.ap().opt()], outs=[cc_out.ap().opt()]), inc=sems["cc"], k=1)
    x0 = p.sb("x0", [128, FC], BF16)
    op("gpsimd", lambda g: g.dma_start(
        out=x0[:, :], in_=cc_out.ap()[0:FC, :].rearrange("f p -> p f"),
        allow_slow_non_contiguous=True), inc=sems["gp"])

    for fc in range(FC):
        op("tensor", lambda t, fc=fc: t.matmul(
            pp["bc1"][0:8, 0:1], rw_bf[:, fc * 8:(fc + 1) * 8], x0[:, fc:fc + 1],
            start=(fc == 0), stop=False), inc=sems["pe"])
    brb = p.sb("brb", [1, 8], BF16)
    onesb = p.sb("onesb", [1, 1], BF16)
    op("vector", lambda v: v.tensor_copy(brb[:, :], br_sb[:, :]), inc=sems["dve"])
    op("vector", lambda v: v.memset(onesb[:, :], 1.0), inc=sems["dve"])
    op("tensor", lambda t: t.matmul(pp["bc1"][0:8, 0:1], brb[0:1, 0:8], onesb[0:1, 0:1],
                                    start=False, stop=True), inc=sems["pe"])
    sc_col = p.sb("sc_col", [8, 1], F32)
    op("scalar", lambda s: s.activation(sc_col[0:8, 0:1], pp["bc1"][0:8, 0:1], AF.Copy),
       inc=sems["act"])
    sc_row = p.sb("sc_row", [1, 16], F32)
    op("tensor", lambda t: t.transpose(pp["bc0"][0:1, 0:8], sc_col[0:8, 0:1],
                                       C["ident_f32"][0:8, 0:8]), inc=sems["pe"])
    op("scalar", lambda s: s.activation(sc_row[0:1, 0:8], pp["bc0"][0:1, 0:8], AF.Copy),
       inc=sems["act"])
    sm = p.sb("sm", [1, 8], F32)
    ecols = p.sb("ecols", [128, 2], F32)
    for which in range(2):
        op("vector", lambda v: v.reduce_max(sm[0:1, 0:1], sc_row[0:1, 0:8], axis=AX.X),
           inc=sems["dve"])
        op("vector", lambda v: v.tensor_scalar(sc_row[0:1, 8:16], sc_row[0:1, 0:8],
                                               sm[0:1, 0:1], None, ALU.is_ge), inc=sems["dve"])
        op("vector", lambda v: v.tensor_mul(sm[0:1, 0:8] if False else sc_row[0:1, 8:16],
                                            sc_row[0:1, 8:16], C["iota8"][0:1, :]), inc=sems["dve"])
        op("vector", lambda v: v.reduce_sum(sm[0:1, 2:3], sc_row[0:1, 8:16], axis=AX.X),
           inc=sems["dve"])
        op("tensor", lambda t: t.matmul(pp["bc0"][:, 0:1], C["ones_row"][0:1, :],
                                        sm[0:1, 2:3], start=True, stop=True), inc=sems["pe"])
        op("scalar", lambda s, w=which: s.activation(ecols[:, w:w + 1], pp["bc0"][:, 0:1],
                                                     AF.Copy), inc=sems["act"])
        op("vector", lambda v: v.tensor_scalar(sc_row[0:1, 8:16], sc_row[0:1, 0:8],
                                               sm[0:1, 0:1], None, ALU.is_ge), inc=sems["dve"])
        op("vector", lambda v: v.tensor_scalar(sc_row[0:1, 8:16], sc_row[0:1, 8:16], 1e30,
                                               None, ALU.mult), inc=sems["dve"])
        op("vector", lambda v: v.tensor_sub(sc_row[0:1, 0:8], sc_row[0:1, 0:8],
                                            sc_row[0:1, 8:16]), inc=sems["dve"])

    wf = p.sb("f0wf", [128, D], F32)
    wb = p.sb("f0wb", [128, D], BF16)
    wb2c = p.sb("f0wb2c", [128, D], BF16)
    h1 = p.sb("h1", [128, FC * (T // 2)], BF16)
    h2b = p.sb("h2b", [128, FC * (T // 2)], BF16)
    brow = p.sb("f0brow", [1, D], F32)
    tmp = p.sb("f0tmp", [128, NC], F32)
    idxc = p.sb("idxc", [128, 1], I32)
    idxf = p.sb("idxf", [128, 1], F32)
    bidx = p.sb("bidx", [1, 1], I32)

    def gather_w(w_dram_, sel, mc, dst):
        op("vector", lambda v, sel=sel: v.tensor_scalar(
            idxf[:, 0:1], ecols[:, sel:sel + 1], 1024.0, None, ALU.mult), inc=sems["dve"])
        op("vector", lambda v, mc=mc: v.tensor_scalar(
            idxf[:, 0:1], idxf[:, 0:1], float(mc * 128), None, ALU.add), inc=sems["dve"])
        op("vector", lambda v: v.tensor_add(idxf[:, 0:1], idxf[:, 0:1], C["iota128"][:, :]),
           inc=sems["dve"])
        op("vector", lambda v: v.tensor_copy(idxc[:, 0:1], idxf[:, 0:1]), inc=sems["dve"])
        op("gpsimd", lambda g, w=w_dram_, dst=dst: g.indirect_dma_start(
            dst[:, 0:D], None, w.ap(), IndirectOffsetOnAxis(idxc[:, 0:1], 0)),
           inc=sems["gp"])

    def gather_b(b_dram_, sel):
        op("vector", lambda v, sel=sel: v.tensor_copy(idxc[:, 0:1], ecols[:, sel:sel + 1]),
           inc=sems["dve"])
        op("gpsimd", lambda g, b=b_dram_: g.indirect_dma_start(
            wf[:, 0:D], None, b.ap(), IndirectOffsetOnAxis(idxc[:, 0:1], 0)),
           inc=sems["gp"])
        op("vector", lambda v: v.tensor_copy(brow[0:1, 0:D], wf[0:1, 0:D]), inc=sems["dve"])

    brow2 = p.sb("f0brow2", [1, D], F32)
    HW = T // 2
    for th in range(2):
        toff = th * HW
        for sel, hbuf in ((0, h1), (1, h2b)):
            gather_b(b1_d, sel)
            for mc in range(FC):
                gather_w(w1_d, sel, mc, wf)
                op("vector", lambda v: v.tensor_copy(wb[:, 0:D], wf[:, 0:D]), inc=sems["dve"])
                for i in range(HW // NC):
                    for kc in range(FC):
                        lhsT = wb[:, kc * 128:(kc + 1) * 128]
                        rhs = xn[:, kc * T + toff + i * NC: kc * T + toff + (i + 1) * NC]
                        op("tensor", lambda t, lhsT=lhsT, rhs=rhs, f0=(kc == 0):
                           t.matmul(pp["mm0"][:, :], lhsT, rhs, start=f0, stop=False))
                    op("tensor", lambda t, br=brow[0:1, 128 * mc:128 * (mc + 1)]:
                       t.matmul(pp["mm0"][:, :], br, C["ones_rowN"][0:1, 0:NC],
                                start=False, stop=True), inc=sems["pe"])
                    op("scalar", lambda s, o=fm_view(hbuf, mc, HW, i * NC, NC):
                       s.activation(o, pp["mm0"][:, :], AF.Gelu), inc=sems["act"])
        gather_b(b2_d, 0)
        op("vector", lambda v: v.tensor_copy(brow2[0:1, 0:D], brow[0:1, 0:D]), inc=sems["dve"])
        gather_b(b2_d, 1)
        op("vector", lambda v: v.tensor_add(brow2[0:1, 0:D], brow2[0:1, 0:D], brow[0:1, 0:D]),
           inc=sems["dve"])
        for mc in range(FC):
            gather_w(w2_d, 0, mc, wf)
            op("vector", lambda v: v.tensor_copy(wb[:, 0:D], wf[:, 0:D]), inc=sems["dve"])
            gather_w(w2_d, 1, mc, wf)
            op("vector", lambda v: v.tensor_copy(wb2c[:, 0:D], wf[:, 0:D]), inc=sems["dve"])
            for i in range(HW // NC):
                for kc in range(FC):
                    op("tensor", lambda t, lhsT=wb[:, kc * 128:(kc + 1) * 128],
                       rhs=h1[:, kc * HW + i * NC: kc * HW + (i + 1) * NC], f0=(kc == 0):
                       t.matmul(pp["mm1"][:, :], lhsT, rhs, start=f0, stop=False))
                for kc in range(FC):
                    op("tensor", lambda t, lhsT=wb2c[:, kc * 128:(kc + 1) * 128],
                       rhs=h2b[:, kc * HW + i * NC: kc * HW + (i + 1) * NC]:
                       t.matmul(pp["mm1"][:, :], lhsT, rhs, start=False, stop=False))
                op("tensor", lambda t, br=brow2[0:1, 128 * mc:128 * (mc + 1)]:
                   t.matmul(pp["mm1"][:, :], br, C["ones_rowN"][0:1, 0:NC],
                            start=False, stop=True), inc=sems["pe"])
                op("scalar", lambda s: s.activation(tmp[:, 0:NC], pp["mm1"][:, :], AF.Copy,
                                                    scale=0.25), inc=sems["act"])
                xa = x_sb[:, mc * T + toff + i * NC: mc * T + toff + (i + 1) * NC]
                op("vector", lambda v, xa=xa: v.tensor_add(xa, xa, tmp[:, 0:NC]), inc=sems["dve"])

    finish_frame(p, x_sb, y_d, T, sems, barrier(p, sems))
    return p.build()


def prep_ffn0(inp):
    d = {}
    rw = inp["L0_ffn_Wr"]
    d["rw"] = np.ascontiguousarray(rw.reshape(FC, 128, 8).transpose(1, 0, 2).reshape(128, FC * 8))
    d["br"] = inp["L0_ffn_br"].reshape(1, 8)
    d["w1"] = np.concatenate([mblocks(inp["L0_ffn_W1"][e]).reshape(FC * 128, D) for e in range(8)])
    d["w2"] = np.concatenate([mblocks(inp["L0_ffn_W2"][e]).reshape(FC * 128, D) for e in range(8)])
    d["b1"] = np.ascontiguousarray(inp["L0_ffn_b1"])
    d["b2"] = np.ascontiguousarray(inp["L0_ffn_b2"])
    return d


FFN0_CONSTS = ("ones_col", "ones_row", "ones_rowN", "ident_bf", "ident_f32", "iota8", "iota128")


def run_ffn0(x_shards, prep, consts):
    nc = get_prog("ffn0", build_ffn0)
    in_maps = []
    for c in range(NCORE):
        m = {"x": x_shards[c]}
        m.update(prep)
        feed_consts(m, FFN0_CONSTS, consts)
        in_maps.append(m)
    res = run_bass_kernel_spmd(nc, in_maps, core_ids=list(range(NCORE)))
    return [res.results[c]["y"] for c in range(NCORE)]



def build_attn1():
    p = Prog("attn1")
    nc = p.nc
    W = S
    nch = W // NC
    sems = {k: p.sem(k) for k in ("pe", "act", "dve", "ld", "st", "cl", "gp")}
    x_d = nc.dram_tensor("x", [D, W], F32, kind="ExternalInput")
    ho_d = nc.dram_tensor("hout", [512, W], BF16, kind="ExternalOutput")
    wq_d = nc.dram_tensor("wq", [4, 128, D], F32, kind="ExternalInput")
    wk_d = nc.dram_tensor("wk", [4, 128, D], F32, kind="ExternalInput")
    wv_d = nc.dram_tensor("wv", [4, 128, D], F32, kind="ExternalInput")
    wp_d = nc.dram_tensor("wp", [4, 128, D], F32, kind="ExternalInput")
    pet_d = nc.dram_tensor("pet", [FC, 128, W], F32, kind="ExternalInput")
    bq_d = nc.dram_tensor("bq", [1, 512], F32, kind="ExternalInput")
    bk_d = nc.dram_tensor("bk", [1, 512], F32, kind="ExternalInput")
    bv_d = nc.dram_tensor("bv", [1, 512], F32, kind="ExternalInput")
    ub_d = nc.dram_tensor("ub", [1, 512], F32, kind="ExternalInput")
    vb_d = nc.dram_tensor("vb", [1, 512], F32, kind="ExternalInput")
    lng_d = nc.dram_tensor("lng", [128, FC], F32, kind="ExternalInput")
    lnb_d = nc.dram_tensor("lnb", [128, FC], F32, kind="ExternalInput")
    p_dram = dram_tmp(nc, "p_dram", [4, 128, W], F32)
    b_dram = dram_tmp(nc, "b_dram", [128, 2 * W + 32], BF16)

    C = Consts(p, sems["cl"], which=("ones_col", "ones_row", "ones_rowN", "ident_bf",
                                     "ident_f32"))
    pp = {"st0": p.psum("pst0", [1, NC]), "st1": p.psum("pst1", [1, NC]),
          "bc0": p.psum("pbc0", [128, NC]), "bc1": p.psum("pbc1", [128, NC]),
          "mm0": p.psum("pmm0", [128, NC]), "mm1": p.psum("pmm1", [128, NC]),
          "av": p.psum("pav", [65, 128]), "tr": p.psum("ptr", [128, 128], BF16)}
    pe, act, dve, ld = sems["pe"], sems["act"], sems["dve"], sems["ld"]

    rows = p.sb("rows", [1, 512 * 4], F32)
    for dr, off in ((bq_d, 0), (bk_d, 1024), (bv_d, 1536), (ub_d, 512), (vb_d, None)):
        pass
    bqu = rows[0:1, 0:512]
    vmu = rows[0:1, 512:1024]
    bkr = rows[0:1, 1024:1536]
    bvr = rows[0:1, 1536:2048]
    tmpr = p.sb("tmpr", [1, 512], F32)
    p.op("sync", lambda sy: sy.dma_start(out=bqu, in_=bq_d[0:1, :]), inc=ld, k=16)
    p.op("sync", lambda sy: sy.dma_start(out=bkr, in_=bk_d[0:1, :]), inc=ld, k=16)
    p.op("sync", lambda sy: sy.dma_start(out=bvr, in_=bv_d[0:1, :]), inc=ld, k=16)
    p.op("sync", lambda sy: sy.dma_start(out=vmu, in_=vb_d[0:1, :]), inc=ld, k=16)
    p.op("sync", lambda sy: sy.dma_start(out=tmpr[0:1, :], in_=ub_d[0:1, :]), inc=ld, k=16)
    lng_sb = p.sb("lng_sb", [128, FC], F32)
    lnb_sb = p.sb("lnb_sb", [128, FC], F32)
    p.op("sync", lambda sy: sy.dma_start(out=lng_sb[:, :], in_=lng_d[:, :]), inc=ld, k=16)
    p.op("sync", lambda sy: sy.dma_start(out=lnb_sb[:, :], in_=lnb_d[:, :]), inc=ld, k=16)
    c_t = ld.n
    for e in ("vector", "tensor", "scalar", "gpsimd"):
        p.wait(e, (sems["cl"], sems["cl"].n), (ld, c_t))
    p.chain("vector", dve, lambda v: v.tensor_sub(vmu, vmu, tmpr[0:1, :]))
    p.chain("vector", dve, lambda v: v.tensor_add(bqu, bqu, tmpr[0:1, :]))

    big = p.sb("big", [128, FC * W], BF16)
    xch = p.sb("xch", [128, FC * NC], F32)
    pef = xch
    wf = p.sb("awf", [128, D], F32)
    wb = p.sb("awb", [128, D], BF16)
    stg = p.sb("stg", [128, NC], F32)
    for kc in range(FC):
        p.op("sync", lambda sy, kc=kc: sy.dma_start(out=pef[:, :], in_=pet_d[kc, :, :]),
             waits=barrier(p, sems, ("dve",)), inc=ld, k=16)
        l_t = ld.n
        p.chain("vector", dve, lambda v, kc=kc, o=big[:, kc * W:(kc + 1) * W]:
                v.tensor_copy(o, pef[:, :]), extra_waits=[(ld, l_t)])
    pe_done = dve.n
    for mc in range(4):
        p.op("sync", lambda sy, mc=mc: sy.dma_start(out=wf[:, :], in_=wp_d[mc, :, :]),
             waits=barrier(p, sems, ("dve",)), inc=ld, k=16)
        l_t = ld.n
        p.chain("vector", dve, lambda v: v.tensor_copy(wb[:, :], wf[:, :]),
                extra_waits=[(ld, l_t), (dve, pe_done)])
        cv = dve.n
        for i in range(nch):
            ps = pp["mm0"] if i % 2 == 0 else pp["mm1"]
            for kc in range(FC):
                lhsT = wb[:, kc * 128:(kc + 1) * 128]
                rhs = big[:, kc * W + i * NC: kc * W + (i + 1) * NC]
                p.op("tensor", lambda t, lhsT=lhsT, rhs=rhs, ps=ps, f0=(kc == 0),
                     f1=(kc == FC - 1): t.matmul(ps[:, :], lhsT, rhs, start=f0, stop=f1),
                     waits=([(dve, cv), (sems["act"], act.n)] if kc == 0 else []),
                     inc=(pe if kc == FC - 1 else None))
            p.chain("scalar", act, lambda s, ps=ps: s.activation(stg[:, :], ps[:, :], AF.Copy),
                    extra_waits=[(pe, pe.n), (sems["gp"], sems["gp"].n)])
            a_t = act.n
            p.op("gpsimd", lambda g, mc=mc, i=i: g.dma_start(
                out=p_dram.ap()[mc, :, i * NC:(i + 1) * NC], in_=stg[:, :]),
                waits=[(act, a_t)], inc=sems["gp"], k=16)
    pdram_done = [(sems["gp"], sems["gp"].n)]

    xn = big
    for i in range(nch):
        ap3 = x_d.ap()[:, i * NC:(i + 1) * NC].rearrange("(f p) t -> p f t", p=128)
        p.op("sync", lambda sy, a=ap3: sy.dma_start(out=fm_sb_ap(xch, NC), in_=a),
             waits=barrier(p, sems) + pdram_done, inc=ld, k=16)
        lt = ld.n
        for e in ("vector", "tensor", "scalar"):
            p.wait(e, (ld, lt))
        emit_ln_q(p, f"a1ln{i}", xch, NC, 0, NC, None, C, pp, sems,
                  affine=(lng_sb, lnb_sb), xn_out=(xn, i * NC, W))

    qc_a = p.sb("qc_a", [128, W], BF16)
    qp_a = p.sb("qp_a", [128, W], BF16)
    ka_ = p.sb("ka_", [128, W], BF16)
    va_ = p.sb("va_", [128, W], BF16)
    vtm = p.sb("vtm", [128, 32 * 65], BF16)
    phead = p.sb("phead", [65, W + 128], BF16)
    qcur = p.sb("qcur", [64, W], BF16)
    qpc = p.sb("qpc", [64, W + 128], BF16)
    kcur = p.sb("kcur", [64, W], BF16)
    vcur = p.sb("vcur", [64, W], BF16)
    stgb = [p.sb("stgb0", [128, NC], BF16), p.sb("stgb1", [128, NC], BF16)]
    posb = p.sb("posb", [128, W], BF16)
    ptile = p.sb("ptile", [128, W], BF16)
    ptt = p.sb("ptt", [128, 128], BF16)
    maxb = p.sb("a1maxb", [128, 8], F32)
    maxc = p.sb("a1maxc", [128, 1], F32)
    sumb = p.sb("a1sumb", [128, 8], F32)
    rrow = p.sb("a1rrow", [1, 128], F32)
    stage = p.sb("a1stage", [128, 128], BF16)
    tmpc = p.sb("a1tmpc", [128, 128], F32)

    def proj(tag, w_dr, dst, ch2, brow_ap, scale, extra_evac=None):
        pre = barrier(p, sems)
        p.op("sync", lambda sy: sy.dma_start(out=wf[:, :], in_=w_dr[ch2, :, :]),
             waits=pre, inc=ld, k=16)
        l_t = ld.n
        p.chain("vector", dve, lambda v: v.tensor_copy(wb[:, :], wf[:, :]),
                extra_waits=[(ld, l_t)] + pre)
        cv = dve.n
        for i in range(nch):
            ps = pp["mm0"] if i % 2 == 0 else pp["mm1"]
            for kc in range(FC):
                lhsT = wb[:, kc * 128:(kc + 1) * 128]
                rhs = xn[:, kc * W + i * NC: kc * W + (i + 1) * NC]
                p.op("tensor", lambda t, lhsT=lhsT, rhs=rhs, ps=ps, f0=(kc == 0):
                     t.matmul(ps[:, :], lhsT, rhs, start=f0, stop=False),
                     waits=([(dve, cv), (act, act.n)] if kc == 0 else []))
            p.op("tensor", lambda t, ps=ps, br=brow_ap: t.matmul(
                ps[:, :], br, C["ones_rowN"][0:1, 0:NC], start=False,
                stop=(extra_evac is None)), inc=pe)
            m1 = pe.n
            p.chain("scalar", act, lambda s, o=dst[:, i * NC:(i + 1) * NC], ps=ps, sc=scale:
                    s.activation(o, ps[:, :], AF.Copy, scale=sc),
                    extra_waits=[(pe, m1)])
            if extra_evac is not None:
                dst2, brow2_ap = extra_evac
                a_t = act.n
                p.op("tensor", lambda t, ps=ps, br=brow2_ap: t.matmul(
                    ps[:, :], br, C["ones_rowN"][0:1, 0:NC], start=False, stop=True),
                    waits=[(act, a_t)], inc=pe)
                m2 = pe.n
                p.chain("scalar", act, lambda s, o=dst2[:, i * NC:(i + 1) * NC], ps=ps, sc=scale:
                        s.activation(o, ps[:, :], AF.Copy, scale=sc),
                        extra_waits=[(pe, m2)])

    SCL = 0.125
    for ch2 in range(4):
        proj("q", wq_d, qc_a, ch2, bqu[0:1, ch2 * 128:(ch2 + 1) * 128], SCL,
             extra_evac=(qp_a, vmu[0:1, ch2 * 128:(ch2 + 1) * 128]))
        proj("k", wk_d, ka_, ch2, bkr[0:1, ch2 * 128:(ch2 + 1) * 128], 1.0)
        proj("v", wv_d, va_, ch2, bvr[0:1, ch2 * 128:(ch2 + 1) * 128], 1.0)
        for sub in range(2):
            hp = 2 * ch2 + sub
            emit_head_xl(p, hp, qc_a, qp_a, ka_, va_, vtm, phead, qcur, qpc, kcur, vcur,
                         stgb, posb, ptile, ptt, maxb, maxc, sumb, rrow, stage, tmpc,
                         xch, p_dram, b_dram, ho_d, C, pp, sems)
    p.wait("gpsimd", (sems["st"], sems["st"].n))
    return p.build()


def emit_head_xl(p, hp, qc_a, qp_a, ka_, va_, vtm, phead, qcur, qpc, kcur, vcur,
                 stgb, posb, ptile, ptt, maxb, maxc, sumb, rrow, stage, tmpc,
                 sqbuf, p_dram, b_dram, ho_d, C, pp, sems):
    W = S
    nch = W // NC
    pe, act, dve, ld, gp = sems["pe"], sems["act"], sems["dve"], sems["ld"], sems["gp"]
    base = 64 * (hp % 2)
    ch2 = hp // 2
    pre = barrier(p, sems) + [(gp, gp.n), (sems["st"], sems["st"].n)]
    for dst, srcb in ((qcur, qc_a), (kcur, ka_), (vcur, va_)):
        p.op("sync", lambda sy, o=dst[0:64, 0:W], a=srcb[base:base + 64, 0:W]:
             sy.dma_start(out=o, in_=a), waits=pre, inc=ld, k=16)
    p.op("sync", lambda sy, o=qpc[0:64, 0:W], a=qp_a[base:base + 64, 0:W]:
         sy.dma_start(out=o, in_=a), waits=pre, inc=ld, k=16)
    p.op("gpsimd", lambda g: g.dma_start(
        out=phead[0:64, 0:W], in_=p_dram.ap()[ch2, base:base + 64, :]), waits=pre,
        inc=ld, k=16)
    cp_t = ld.n
    p.chain("vector", dve, lambda v: v.memset(qpc[0:64, W:W + 128], 0.0), extra_waits=pre)
    p.chain("vector", dve, lambda v: v.memset(vtm[:, :], 1.0))
    ms_t = dve.n
    for kt in range(32):
        vsl = vcur[0:64, kt * 128:(kt + 1) * 128]
        p.op("tensor", lambda t, a=vsl: t.transpose(pp["tr"][:, 0:64], a,
                                                    C["ident_bf"][0:64, 0:64]),
             waits=[(dve, ms_t), (ld, cp_t)] + ([(act, act.n)] if kt else []), inc=pe)
        p.chain("scalar", act, lambda s, kt=kt: s.activation(
            vtm[:, kt * 65: kt * 65 + 64], pp["tr"][:, 0:64], AF.Copy),
            extra_waits=[(pe, pe.n)])
    vtm_done = act.n

    for qt in range(32):
        q0 = qt * 128
        stg_free = [[], []]
        for half, qoff, boff in ((0, 0, 0), (1, 1, W + 1)):
            for i in range(nch):
                ps = pp["mm0"] if i % 2 == 0 else pp["mm1"]
                j = i % 2
                lhsT = qpc[0:64, q0 + qoff: q0 + qoff + 128]
                rhs = phead[0:64, i * NC:(i + 1) * NC]
                p.op("tensor", lambda t, lhsT=lhsT, rhs=rhs, ps=ps: t.matmul(
                    ps[:, :], lhsT, rhs, start=True, stop=True),
                    waits=[(act, act.n), (ld, cp_t)], inc=pe)
                p.chain("scalar", act, lambda s, o=stgb[j], ps=ps:
                        s.activation(o[:, :], ps[:, :], AF.Copy),
                        extra_waits=[(pe, pe.n)] + stg_free[j])
                a_t = act.n
                p.op("gpsimd", lambda g, j=j, boff=boff, i=i: g.dma_start(
                    out=b_dram.ap()[:, boff + i * NC: boff + (i + 1) * NC], in_=stgb[j][:, :]),
                    waits=[(act, a_t)], inc=gp, k=16)
                stg_free[j] = [(gp, gp.n)]
        g_t = gp.n
        diag = bass.AP(b_dram, (S - 1 - q0), [[2 * W + 31, 128], [1, W]])
        p.op("gpsimd", lambda g, diag=diag: g.dma_start(out=posb[:, :], in_=diag),
             waits=[(gp, g_t)], inc=gp, k=16)
        pos_t = gp.n
        for i in range(nch):
            ps = pp["mm0"] if i % 2 == 0 else pp["mm1"]
            lhsT = qcur[0:64, q0:q0 + 128]
            rhs = kcur[0:64, i * NC:(i + 1) * NC]
            p.op("tensor", lambda t, lhsT=lhsT, rhs=rhs, ps=ps: t.matmul(
                ps[:, :], lhsT, rhs, start=True, stop=False),
                waits=[(dve, dve.n), (gp, pos_t)])
            p.op("tensor", lambda t, ps=ps, pb=posb[:, i * NC:(i + 1) * NC]: t.matmul(
                ps[:, :], C["ident_bf"][:, :], pb, start=False, stop=True), inc=pe)
            p.chain("vector", dve, lambda v, o=maxb[:, i:i + 1], ps=ps:
                    v.reduce_max(o, ps[:, :], axis=AX.X), extra_waits=[(pe, pe.n)])
        p.chain("vector", dve, lambda v: v.reduce_max(maxc[:, 0:1], maxb[:, :], axis=AX.X))
        p.chain("vector", dve, lambda v: v.tensor_scalar(maxc[:, 0:1], maxc[:, 0:1], -1.0,
                                                         None, ALU.mult))
        mx_t = dve.n
        for i in range(nch):
            ps = pp["mm0"] if i % 2 == 0 else pp["mm1"]
            lhsT = qcur[0:64, q0:q0 + 128]
            rhs = kcur[0:64, i * NC:(i + 1) * NC]
            p.op("tensor", lambda t, lhsT=lhsT, rhs=rhs, ps=ps: t.matmul(
                ps[:, :], lhsT, rhs, start=True, stop=False),
                waits=[(act, act.n), (dve, mx_t)])
            p.op("tensor", lambda t, ps=ps, pb=posb[:, i * NC:(i + 1) * NC]: t.matmul(
                ps[:, :], C["ident_bf"][:, :], pb, start=False, stop=True), inc=pe)
            p.chain("scalar", act, lambda s, o=ptile[:, i * NC:(i + 1) * NC], ps=ps,
                    ac=sumb[:, i:i + 1]: s.activation(o, ps[:, :], AF.Exp,
                                                      bias=maxc[:, 0:1], accum_out=ac),
                    extra_waits=[(pe, pe.n)])
        e_t = act.n
        p.chain("vector", dve, lambda v: v.reduce_sum(maxc[:, 0:1], sumb[:, :], axis=AX.X),
                extra_waits=[(act, e_t)])
        p.chain("vector", dve, lambda v: v.reciprocal(maxc[:, 0:1], maxc[:, 0:1]))
        rc_t = dve.n
        for kt in range(32):
            psl = ptile[:, kt * 128:(kt + 1) * 128]
            p.op("tensor", lambda t, a=psl: t.transpose(pp["tr"][:, :], a, C["ident_bf"][:, :]),
                 waits=[(act, act.n)], inc=pe)
            tr_t = pe.n
            p.chain("scalar", act, lambda s: s.activation(ptt[:, :], pp["tr"][:, :], AF.Copy),
                    extra_waits=[(pe, tr_t)])
            a_t = act.n
            p.op("tensor", lambda t, lhsT=vtm[:, kt * 65:(kt + 1) * 65],
                 f0=(kt == 0), f1=(kt == 31): t.matmul(
                     pp["av"][:, :], lhsT, ptt[:, :], start=f0, stop=f1),
                 waits=[(act, a_t)] + ([(act, vtm_done)] if kt == 0 else []), inc=pe)
        av_t = pe.n
        p.op("tensor", lambda t: t.transpose(pp["bc0"][0:1, 0:128], maxc[:, 0:1],
                                             C["ident_f32"][:, :]),
             waits=[(dve, rc_t), (pe, av_t), (act, act.n)], inc=pe)
        p.chain("scalar", act, lambda s: s.activation(rrow[0:1, :], pp["bc0"][0:1, 0:128],
                                                      AF.Copy), extra_waits=[(pe, pe.n)])
        rr_t = act.n
        p.op("tensor", lambda t: t.matmul(pp["bc1"][0:64, 0:128], C["ones_row"][0:1, 0:64],
                                          rrow[0:1, :], start=True, stop=True),
             waits=[(rr_t and (act, rr_t))] if False else [(act, rr_t)], inc=pe)
        b2_t = pe.n
        p.chain("scalar", act, lambda s: s.activation(tmpc[0:64, :], pp["av"][0:64, :],
                                                      AF.Copy), extra_waits=[(pe, b2_t)])
        a2 = act.n
        p.chain("vector", dve, lambda v: v.tensor_mul(stage[0:64, :], tmpc[0:64, :],
                                                      pp["bc1"][0:64, 0:128]),
                extra_waits=[(act, a2), (sems["st"], sems["st"].n)])
        m_t = dve.n
        dst = ho_d.ap()[hp * 64:(hp + 1) * 64, qt * 128:(qt + 1) * 128]
        p.op("gpsimd", lambda g, o=dst: g.dma_start(out=o, in_=stage[0:64, :]),
             waits=[(dve, m_t)], inc=sems["st"], k=16)


def prep_attn1(inp, half):
    d = {}
    heads = range(8 * half, 8 * half + 8)
    cols = slice(512 * half, 512 * half + 512)
    d["wq"] = mblocks(inp["L1_attn_Wq"][:, cols])
    d["wk"] = mblocks(inp["L1_attn_Wk"][:, cols])
    d["wv"] = mblocks(inp["L1_attn_Wv"][:, cols])
    d["wp"] = mblocks(inp["L1_attn_Wp"][:, cols])
    d["bq"] = inp["L1_attn_bq"][cols].reshape(1, 512)
    d["bk"] = inp["L1_attn_bk"][cols].reshape(1, 512)
    d["bv"] = inp["L1_attn_bv"][cols].reshape(1, 512)
    d["ub"] = inp["L1_attn_u_bias"].reshape(1024)[cols].reshape(1, 512)
    d["vb"] = inp["L1_attn_v_bias"].reshape(1024)[cols].reshape(1, 512)
    d["lng"] = np.ascontiguousarray(inp["L1_attn_ln_g"].reshape(FC, 128).T)
    d["lnb"] = np.ascontiguousarray(inp["L1_attn_ln_b"].reshape(FC, 128).T)
    pos = np.arange(S, dtype=np.float64)[:, None]
    div = np.exp(np.arange(0, D, 2, dtype=np.float64) * (-math.log(10000.0) / D))
    ang = pos * div
    pet = np.stack([np.sin(ang), np.cos(ang)], axis=-1).reshape(S, D).astype(np.float32)
    d["pet"] = np.ascontiguousarray(pet.T.reshape(FC, 128, S))
    return d


def run_attn1(x, preps, consts):
    nc = get_prog("attn1", build_attn1)
    names = ("ones_col", "ones_row", "ones_rowN", "ident_bf", "ident_f32")
    in_maps = []
    for c in range(NCORE):
        b, half = c // 2, c % 2
        m = {"x": np.ascontiguousarray(x[b].T)}
        m.update(preps[half])
        feed_consts(m, names, consts)
        in_maps.append(m)
    res = run_bass_kernel_spmd(nc, in_maps, core_ids=list(range(NCORE)))
    return [res.results[c]["hout"] for c in range(NCORE)]
